# version 02
# baseline (speedup 1.0000x reference)
import numpy as np
from contextlib import ExitStack

import concourse.bass as bass
import concourse.mybir as mybir
from concourse.bass_utils import run_bass_kernel_spmd

F32 = mybir.dt.float32
BF16 = mybir.dt.bfloat16
I32 = mybir.dt.int32
AF = mybir.ActivationFunctionType
ALU = mybir.AluOpType
AXX = mybir.AxisListType.X

NCORE = 8
B, S, DM = 4, 2048, 4096
TPC = S * B // NCORE
NFC = DM // 128
INVSQ128 = 1.0 / float(np.sqrt(np.float32(128.0)))
RACE = [False]


def _bf16(a):
    import ml_dtypes
    return np.asarray(a).astype(ml_dtypes.bfloat16)


ENGINES = ("sync", "vector", "scalar", "tensor", "gpsimd")


class Prog:
    N_DMA_SEMS = 14

    def __init__(self, nc, stack):
        self.nc = nc
        self.items = {e: [] for e in ENGINES}
        self.eng_sems = {e: stack.enter_context(nc.semaphore("se_" + e)) for e in ENGINES}
        self.dma_sems = [stack.enter_context(nc.semaphore("sd%d" % i))
                         for i in range(self.N_DMA_SEMS)]
        self.dma_cum = [0] * self.N_DMA_SEMS
        self.dma_rr = 0

    def _norm(self, deps):
        out = []
        for d in deps:
            if d is None:
                continue
            if isinstance(d, list):
                out.extend(x for x in d if x is not None)
            else:
                out.append(d)
        return out

    def ins(self, eng, fn, deps=()):
        idx = len(self.items[eng])
        self.items[eng].append({"fn": fn, "deps": self._norm(deps), "dma": None})
        return ("e", eng, idx)

    def dma(self, eng, fn, deps=(), sem_slot=None):
        if sem_slot is None:
            base = {"sync": 0, "scalar": 7, "gpsimd": 11, "vector": 11,
                    "tensor": 11}[eng]
            n = {"sync": 7, "scalar": 4, "gpsimd": 3, "vector": 3, "tensor": 3}[eng]
            self.dma_rr_q = getattr(self, "dma_rr_q", {})
            r = self.dma_rr_q.get(eng, 0)
            sem_slot = base + (r % n)
            self.dma_rr_q[eng] = r + 1
        self.dma_cum[sem_slot] += 16
        tgt = self.dma_cum[sem_slot]
        idx = len(self.items[eng])
        self.items[eng].append({"fn": fn, "deps": self._norm(deps),
                                "dma": (sem_slot, tgt)})
        return ("d", sem_slot, tgt)

    def last(self, eng):
        n = len(self.items[eng])
        return ("e", eng, n - 1) if n else None

    def seal(self):
        deps = []
        for slot, cum in enumerate(self.dma_cum):
            if cum > 0:
                deps.append(("d", slot, cum))
        for e in ENGINES:
            if e != "gpsimd" and self.items[e]:
                deps.append(("e", e, len(self.items[e]) - 1))
        self.ins("gpsimd", lambda e: e.engine_nop(), deps=deps)

    def build(self, block):
        for e in ENGINES:
            for it in self.items[e]:
                nd = []
                for d in it["deps"]:
                    if d[0] == "e" and self.items[d[1]][d[2]]["dma"] is not None:
                        slot, tgt = self.items[d[1]][d[2]]["dma"]
                        nd.append(("d", slot, tgt))
                    else:
                        nd.append(d)
                it["deps"] = nd
        targets = {e: set() for e in ENGINES}
        for e in ENGINES:
            for it in self.items[e]:
                for d in it["deps"]:
                    if d[0] == "e":
                        targets[d[1]].add(d[2])
        ranks = {e: {i: r + 1 for r, i in enumerate(sorted(targets[e]))} for e in ENGINES}
        import os
        if os.environ.get("PROG_STATS"):
            print("PROG_STATS items:", {e: len(self.items[e]) for e in ENGINES})
            print("PROG_STATS max_rank:", {e: len(ranks[e]) for e in ENGINES})
            print("PROG_STATS dma_cum:", self.dma_cum)
        prog = self

        def mk(e):
            def body(engobj):
                waited = {}
                for k, it in enumerate(prog.items[e]):
                    for d in it["deps"]:
                        if d[0] == "e":
                            _, a, i = d
                            sem, v, key = prog.eng_sems[a], ranks[a][i], ("e", a)
                        else:
                            _, slot, tgt = d
                            sem, v, key = prog.dma_sems[slot], tgt, ("d", slot)
                        if waited.get(key, -1) < v:
                            engobj.wait_ge(sem, v)
                            waited[key] = v
                    instr = it["fn"](engobj)
                    if it["dma"] is not None:
                        slot, tgt = it["dma"]
                        instr.then_inc(prog.dma_sems[slot], 16)
                    elif k in targets[e]:
                        instr.then_inc(prog.eng_sems[e], 1)
            return body

        for e in ENGINES:
            getattr(block, e)(mk(e))


def diag_write_ap(tensor, nrows, ncols, col_shift, row0=0, col0=0):
    full = tensor[:]
    pstep = full.ap[0][0]
    return bass.AP(tensor, row0 * pstep + col0, [[pstep + col_shift, nrows], [1, ncols]])


def rep_ap(psum_slice_tensor, col0, T, reps):
    full = psum_slice_tensor[:]
    pstep = full.ap[0][0]
    return bass.AP(psum_slice_tensor, col0, [[pstep, 128], [0, reps], [1, T]])


def fm_slice_hbm(handle, t0, T):
    return handle.ap()[:, t0:t0 + T].rearrange("(c p) t -> p c t", p=128)


def emit_ln(p, pre_buf, out_buf, bf_buf, sq_buf, ones_bf, ones_row_f, rows,
            ps_stat, ps_bc, T, deps_pre, want_u=False, ln_upto=9, bc_sb=None):
    W = NFC * T
    cast1 = p.ins("scalar", lambda e: e.activation(out=bf_buf[:, 0:W],
                                                   in_=pre_buf[:, 0:W], func=AF.Copy),
                  deps=deps_pre)
    cast2 = p.ins("scalar", lambda e: e.activation(out=sq_buf[:, 0:W],
                                                   in_=pre_buf[:, 0:W], func=AF.Square),
                  deps=deps_pre)
    mm1 = mm2 = None
    for c in range(NFC):
        mm1 = p.ins("tensor", lambda e, c=c: e.matmul(
            out=ps_stat[0:1, 0:T], lhsT=ones_bf[:, 0:1], rhs=bf_buf[:, c * T:(c + 1) * T],
            start=(c == 0), stop=(c == NFC - 1)),
            deps=([cast1] + deps_pre) if c == 0 else [])
    for c in range(NFC):
        mm2 = p.ins("tensor", lambda e, c=c: e.matmul(
            out=ps_stat[0:1, T:2 * T], lhsT=ones_bf[:, 0:1], rhs=sq_buf[:, c * T:(c + 1) * T],
            start=(c == 0), stop=(c == NFC - 1)),
            deps=[cast2] if c == 0 else [])
    m_row, v_row, r_row, u_row = rows
    if ln_upto < 2:
        return mm2, None
    a1 = p.ins("vector", lambda e: e.tensor_single_scalar(
        out=m_row[0:1, 0:T], in_=ps_stat[0:1, 0:T], scalar=1.0 / DM, op=ALU.mult),
        deps=[mm1, mm2])
    a2 = p.ins("vector", lambda e: e.tensor_single_scalar(
        out=v_row[0:1, 0:T], in_=ps_stat[0:1, T:2 * T], scalar=1.0 / DM, op=ALU.mult),
        deps=[mm2])
    a3 = p.ins("vector", lambda e: e.tensor_tensor(
        out=u_row[0:1, 0:T], in0=m_row[0:1, 0:T], in1=m_row[0:1, 0:T], op=ALU.mult),
        deps=[a1])
    a4 = p.ins("vector", lambda e: e.tensor_tensor(
        out=v_row[0:1, 0:T], in0=v_row[0:1, 0:T], in1=u_row[0:1, 0:T], op=ALU.subtract),
        deps=[a2, a3])
    if ln_upto < 3:
        return a4, None
    a4e = p.ins("vector", lambda e: e.tensor_single_scalar(
        out=r_row[0:1, 0:T], in_=v_row[0:1, 0:T], scalar=1e-5, op=ALU.add), deps=[a4])
    a5 = p.ins("scalar", lambda e: e.activation(
        out=r_row[0:1, 0:T], in_=r_row[0:1, 0:T], func=AF.Sqrt), deps=[a4e])
    a6 = p.ins("vector", lambda e: e.reciprocal(out=r_row[0:1, 0:T], in_=r_row[0:1, 0:T]),
               deps=[a5])
    if ln_upto < 4:
        return a6, None
    b1 = p.ins("tensor", lambda e: e.matmul(out=ps_bc[:, 0:T], lhsT=ones_row_f[0:1, 0:128],
                                            rhs=m_row[0:1, 0:T], start=True, stop=True),
               deps=[a1] + deps_pre)
    b2 = p.ins("tensor", lambda e: e.matmul(out=ps_bc[:, T:2 * T], lhsT=ones_row_f[0:1, 0:128],
                                            rhs=r_row[0:1, 0:T], start=True, stop=True),
               deps=[a6])
    if ln_upto < 5:
        return b2, None
    cb1 = p.ins("vector", lambda e: e.tensor_copy(
        out=bc_sb[:, 0:T], in_=ps_bc[:, 0:T]), deps=[b1] + deps_pre)
    cb2 = p.ins("vector", lambda e: e.tensor_copy(
        out=bc_sb[:, T:2 * T], in_=ps_bc[:, T:2 * T]), deps=[b2])
    t1 = p.ins("vector", lambda e: e.tensor_tensor(
        out=out_buf[:, 0:W], in0=pre_buf[:, 0:W], in1=rep_ap(bc_sb, 0, T, NFC),
        op=ALU.subtract), deps=[cb1])
    t2 = p.ins("vector", lambda e: e.tensor_tensor(
        out=out_buf[:, 0:W], in0=out_buf[:, 0:W], in1=rep_ap(bc_sb, T, T, NFC),
        op=ALU.mult), deps=[cb2, t1])
    dep_u = None
    emit_ln.cast2 = cast2
    if want_u:
        c1 = p.ins("vector", lambda e: e.tensor_single_scalar(
            out=u_row[0:1, 0:T], in_=v_row[0:1, 0:T], scalar=1e-5, op=ALU.add), deps=[a4])
        c1b = p.ins("vector", lambda e: e.reciprocal(
            out=u_row[0:1, 0:T], in_=u_row[0:1, 0:T]), deps=[c1])
        c2 = p.ins("vector", lambda e: e.tensor_tensor(
            out=u_row[0:1, 0:T], in0=v_row[0:1, 0:T], in1=u_row[0:1, 0:T], op=ALU.mult),
            deps=[c1b])
        dep_u = c2
    return t2, dep_u


def build_prep():
    nc = bass.Bass("TRN2", target_bir_lowering=False, debug=False,
                   detect_race_conditions=RACE[0])
    ins = {
        "wq0": nc.dram_tensor("wq0", [512, 128 * 32], F32, kind="ExternalInput"),
        "wk0": nc.dram_tensor("wk0", [512, 128], F32, kind="ExternalInput"),
        "wv0": nc.dram_tensor("wv0", [512, 128], F32, kind="ExternalInput"),
        "wq1": nc.dram_tensor("wq1", [512, 4096], F32, kind="ExternalInput"),
        "wk1": nc.dram_tensor("wk1", [512, 1024], F32, kind="ExternalInput"),
        "wv1": nc.dram_tensor("wv1", [512, 1024], F32, kind="ExternalInput"),
        "wo1": nc.dram_tensor("wo1", [512, 4096], F32, kind="ExternalInput"),
    }
    outs = {
        "wqs_bf": nc.dram_tensor("wqs_bf", [512, 128], BF16, kind="ExternalOutput"),
        "wk0_bf": nc.dram_tensor("wk0_bf", [512, 128], BF16, kind="ExternalOutput"),
        "wv0_bf": nc.dram_tensor("wv0_bf", [512, 128], BF16, kind="ExternalOutput"),
        "wq1_bf": nc.dram_tensor("wq1_bf", [512, 4096], BF16, kind="ExternalOutput"),
        "wk1_bf": nc.dram_tensor("wk1_bf", [512, 1024], BF16, kind="ExternalOutput"),
        "wv1_bf": nc.dram_tensor("wv1_bf", [512, 1024], BF16, kind="ExternalOutput"),
        "wo1_bf": nc.dram_tensor("wo1_bf", [512, 4096], BF16, kind="ExternalOutput"),
    }
    with ExitStack() as stack:
        E = stack.enter_context
        buf_f = [E(nc.sbuf_tensor("buf_f%d" % i, [128, 4096], F32)) for i in range(2)]
        buf_b = [E(nc.sbuf_tensor("buf_b%d" % i, [128, 4096], BF16)) for i in range(2)]
        red_f = E(nc.sbuf_tensor("red_f", [128, 128], F32))
        p = Prog(nc, stack)
        block = E(nc.Block())

        last_cast = [None, None]
        last_store = [None, None]
        bi = 0
        jobs = [("wq1", "wq1_bf", 4096, False), ("wo1", "wo1_bf", 4096, False),
                ("wk1", "wk1_bf", 1024, False), ("wv1", "wv1_bf", 1024, False),
                ("wk0", "wk0_bf", 128, False), ("wv0", "wv0_bf", 128, False),
                ("wq0", "wqs_bf", 128, True)]
        for name, oname, ncols, is_red in jobs:
            ih, oh = ins[name], outs[oname if False else oname]
            incols = 4096 if is_red else ncols
            for r in range(4):
                b2 = bi % 2
                bb, ob = buf_f[b2], buf_b[b2]
                ld = p.dma("sync", lambda e, bb=bb, ih=ih, r=r, incols=incols:
                           e.dma_start(out=bb[:, 0:incols],
                                       in_=ih.ap()[r * 128:(r + 1) * 128, :]),
                           deps=[last_cast[b2]])
                if is_red:
                    rd = p.ins("vector", lambda e, bb=bb: e.reduce_sum(
                        out=red_f[:], in_=bb[:].rearrange("p (d h) -> p d h", h=32),
                        axis=AXX), deps=[ld, last_store[b2]])
                    cst = p.ins("vector", lambda e, ob=ob: e.tensor_copy(
                        out=ob[:, 0:128], in_=red_f[:]), deps=[rd])
                else:
                    cst = p.ins("vector", lambda e, bb=bb, ob=ob, ncols=ncols:
                                e.tensor_copy(out=ob[:, 0:ncols], in_=bb[:, 0:ncols]),
                                deps=[ld, last_store[b2]])
                last_cast[b2] = cst
                st = p.dma("scalar", lambda e, oh=oh, ob=ob, r=r, ncols=ncols:
                           e.dma_start(out=oh.ap()[r * 128:(r + 1) * 128, :],
                                       in_=ob[:, 0:ncols]), deps=[cst])
                last_store[b2] = st
                bi += 1
        p.seal()
        p.build(block)
    return nc


def build_l0attn(T_total=TPC, upto=12):
    nc = bass.Bass("TRN2", target_bir_lowering=False, debug=False,
                   detect_race_conditions=RACE[0])
    xT = nc.dram_tensor("xT", [DM, T_total], F32, kind="ExternalInput")
    wqs = nc.dram_tensor("wqs", [DM, 128], BF16, kind="ExternalInput")
    wk0 = nc.dram_tensor("wk0", [DM, 128], BF16, kind="ExternalInput")
    wv0 = nc.dram_tensor("wv0", [DM, 128], BF16, kind="ExternalInput")
    bq0 = nc.dram_tensor("bq0", [128, 32], F32, kind="ExternalInput")
    bk0 = nc.dram_tensor("bk0", [128, 1], F32, kind="ExternalInput")
    bv0 = nc.dram_tensor("bv0", [128, 1], F32, kind="ExternalInput")
    repl = nc.dram_tensor("repl", [128, NFC * 128], BF16, kind="ExternalInput")
    onesf = nc.dram_tensor("onesf", [128, 128], F32, kind="ExternalInput")
    identb = nc.dram_tensor("identb", [128, 128], BF16, kind="ExternalInput")
    x1T = nc.dram_tensor("x1T", [DM, T_total], F32, kind="ExternalOutput")

    NT = T_total // 128
    with ExitStack() as stack:
        E = stack.enter_context
        wqs_s = E(nc.sbuf_tensor("wqs_s", [128, NFC * 128], BF16))
        wk_s = E(nc.sbuf_tensor("wk_s", [128, NFC * 128], BF16))
        wv_s = E(nc.sbuf_tensor("wv_s", [128, NFC * 128], BF16))
        repl_s = E(nc.sbuf_tensor("repl_s", [128, NFC * 128], BF16))
        bq_s = E(nc.sbuf_tensor("bq_s", [128, 32], F32))
        bqs_s = E(nc.sbuf_tensor("bqs_s", [128, 1], F32))
        bk_s = E(nc.sbuf_tensor("bk_s", [128, 1], F32))
        bv_s = E(nc.sbuf_tensor("bv_s", [128, 1], F32))
        ones_f = E(nc.sbuf_tensor("ones_f", [128, 128], F32))
        ident_s = E(nc.sbuf_tensor("ident_s", [128, 128], BF16))
        ones_bf = E(nc.sbuf_tensor("ones_bf", [128, 1], BF16))
        xt_f = [E(nc.sbuf_tensor("xt_f%d" % i, [128, 4096], F32)) for i in range(2)]
        xt_bf = E(nc.sbuf_tensor("xt_bf", [128, 4096], BF16))
        qkv_bf = E(nc.sbuf_tensor("qkv_bf", [128, 384], BF16))
        qsum_tm = E(nc.sbuf_tensor("qsum_tm", [128, 128], BF16))
        k_tm = E(nc.sbuf_tensor("k_tm", [128, 128], BF16))
        kst = [E(nc.sbuf_tensor("kst%d" % i, [32, 128], BF16)) for i in range(2)]
        v_aug = E(nc.sbuf_tensor("v_aug", [128, 256], BF16))
        QD = [E(nc.sbuf_tensor("QD%d" % i, [32, 4096], BF16)) for i in range(2)]
        Pq = [E(nc.sbuf_tensor("Pq%d" % i, [128, 1024], BF16)) for i in range(2)]
        zr = E(nc.sbuf_tensor("zr", [128, 128], F32))
        o_fm = E(nc.sbuf_tensor("o_fm", [128, 128], BF16))
        x1pre = E(nc.sbuf_tensor("x1pre", [128, 4096], F32))
        x1bf = E(nc.sbuf_tensor("x1bf", [128, 4096], BF16))
        xsq = E(nc.sbuf_tensor("xsq", [128, 4096], BF16))
        x1out = E(nc.sbuf_tensor("x1out", [128, 4096], F32))
        rows = tuple(E(nc.sbuf_tensor("row%d" % i, [1, 512], F32)) for i in range(4))
        bc_sb = E(nc.sbuf_tensor("bc_sb", [128, 256], F32))
        ps_qkv = E(nc.psum_tensor("ps_qkv", [128, 384], F32))
        ps_sc = E(nc.psum_tensor("ps_sc", [128, 1024], F32))
        ps_rb = E(nc.psum_tensor("ps_rb", [128, 1024], F32))
        ps_oz = E(nc.psum_tensor("ps_oz", [128, 512], F32))
        ps_tr = E(nc.psum_tensor("ps_tr", [128, 128], BF16))
        p = Prog(nc, stack)
        block = E(nc.Block())

        ld_w = []
        for (dst, src) in [(wqs_s, wqs), (wk_s, wk0), (wv_s, wv0), (repl_s, repl)]:
            ld_w.append(p.dma("sync", lambda e, dst=dst, src=src: e.dma_start(
                out=dst[:].rearrange("p (c d) -> p c d", d=128),
                in_=src.ap().rearrange("(c p) d -> p c d", p=128))))
        ld_b = [p.dma("sync", lambda e: e.dma_start(out=bq_s[:], in_=bq0.ap())),
                p.dma("sync", lambda e: e.dma_start(out=bk_s[:], in_=bk0.ap())),
                p.dma("sync", lambda e: e.dma_start(out=bv_s[:], in_=bv0.ap())),
                p.dma("sync", lambda e: e.dma_start(out=ones_f[:], in_=onesf.ap())),
                p.dma("sync", lambda e: e.dma_start(out=ident_s[:], in_=identb.ap()))]
        bqr = p.ins("vector", lambda e: e.reduce_sum(
            out=bqs_s[:], in_=bq_s[:].rearrange("p (a h) -> p a h", a=1), axis=AXX),
            deps=[ld_b[0]])
        onebf = p.ins("vector", lambda e: e.tensor_copy(out=ones_bf[:], in_=ones_f[:, 0:1]),
                      deps=[ld_b[3]])
        va0 = p.ins("vector", lambda e: e.memset(v_aug[:], 0.0))
        va1 = p.ins("vector", lambda e: e.tensor_copy(
            out=bass.AP(v_aug, 1, [[256, 128], [2, 128]]),
            in_=ones_f[:, 0:1].broadcast_to([128, 128])), deps=[va0, ld_b[3]])
        qd_zero = [p.ins("vector", lambda e, i=i: e.memset(QD[i][:], 0.0)) for i in range(2)]

        prev = {"x1out_st": None, "xt_free": [None, None], "qkv_free": None,
                "pq_free": [None, None], "qd_free": [qd_zero[0], qd_zero[1]],
                "kst_free": [None, None],
                "sc_free": None, "oz_free": None, "vaug_free": va1,
                "ln_prev": None, "pstr_free": None,
                "qkvbf_free": None}

        for it in range(NT):
            bi = it % 2
            ld_x = p.dma("sync", lambda e, bi=bi, it=it: e.dma_start(
                out=xt_f[bi][:].rearrange("p (c t) -> p c t", t=128),
                in_=fm_slice_hbm(xT, it * 128, 128)),
                deps=[prev["xt_free"][bi]])
            cast_x = p.ins("vector", lambda e, bi=bi: e.tensor_copy(
                out=xt_bf[:], in_=xt_f[bi][:]), deps=[ld_x, p.last("tensor")])
            for j, wsb in enumerate((wqs_s, wk_s, wv_s)):
                for c in range(NFC):
                    p.ins("tensor", lambda e, j=j, wsb=wsb, c=c: e.matmul(
                        out=ps_qkv[:, j * 128:(j + 1) * 128],
                        lhsT=wsb[:, c * 128:(c + 1) * 128],
                        rhs=xt_bf[:, c * 128:(c + 1) * 128],
                        start=(c == 0), stop=(c == NFC - 1)),
                        deps=[cast_x, ld_w[0], ld_w[1], ld_w[2], prev["qkv_free"]]
                        if (j == 0 and c == 0) else [])
            mm_qkv = p.last("tensor")
            ev_q = p.ins("scalar", lambda e: e.activation(
                out=qkv_bf[:, 0:128], in_=ps_qkv[:, 0:128], func=AF.Identity,
                bias=bqs_s[:, 0:1]), deps=[mm_qkv, bqr, prev["qkvbf_free"]])
            ev_k = p.ins("scalar", lambda e: e.activation(
                out=qkv_bf[:, 128:256], in_=ps_qkv[:, 128:256], func=AF.Identity,
                bias=bk_s[:, 0:1]), deps=[mm_qkv, ld_b[1]])
            ev_v = p.ins("scalar", lambda e: e.activation(
                out=qkv_bf[:, 256:384], in_=ps_qkv[:, 256:384], func=AF.Identity,
                bias=bv_s[:, 0:1]), deps=[mm_qkv, ld_b[2]])
            prev["qkv_free"] = ev_v
            if upto < 2:
                st_x1 = p.dma("scalar", lambda e, it=it: e.dma_start(
                    out=fm_slice_hbm(x1T, it * 128, 128),
                    in_=x1out[:].rearrange("p (c t) -> p c t", t=128)),
                    deps=[p.last("scalar"), p.last("tensor")])
                prev["x1out_st"] = st_x1
                continue
            tr1 = p.ins("tensor", lambda e: e.transpose(
                out=ps_tr[:], in_=qkv_bf[:, 0:128], identity=ident_s[:]),
                deps=[ev_q, ld_b[4], prev["pstr_free"]])
            cp1 = p.ins("vector", lambda e: e.tensor_copy(out=qsum_tm[:], in_=ps_tr[:]),
                        deps=[tr1] + prev.get("dqd_list", []))
            tr2 = p.ins("tensor", lambda e: e.transpose(
                out=ps_tr[:], in_=qkv_bf[:, 128:256], identity=ident_s[:]),
                deps=[ev_k, cp1])
            cp2 = p.ins("vector", lambda e: e.tensor_copy(out=k_tm[:], in_=ps_tr[:]),
                        deps=[tr2] + prev.get("dkst_list", []))

            va = p.ins("vector", lambda e: e.tensor_copy(
                out=bass.AP(v_aug, 0, [[256, 128], [2, 128]]), in_=qkv_bf[:, 256:384]),
                deps=[ev_v, prev["vaug_free"]])
            if upto < 3:
                st_x1 = p.dma("scalar", lambda e, it=it: e.dma_start(
                    out=fm_slice_hbm(x1T, it * 128, 128),
                    in_=x1out[:].rearrange("p (c t) -> p c t", t=128)),
                    deps=[cp1, cp2, va, p.last("scalar")])
                prev["x1out_st"] = st_x1
                continue
            pv_extracts = []
            dqd_list, dkst_list = [], []
            for g in range(4):
                gb = g % 2
                dqd = p.dma("scalar", lambda e, gb=gb, g=g: e.dma_start(
                    out=diag_write_ap(QD[gb], 32, 128, 128),
                    in_=qsum_tm[g * 32:(g + 1) * 32, :]),
                    deps=[cp1, prev["qd_free"][gb]])
                dqd_list.append(dqd)
                dkst = p.dma("scalar", lambda e, gb=gb, g=g: e.dma_start(
                    out=kst[gb][:], in_=k_tm[g * 32:(g + 1) * 32, :]),
                    deps=[cp2, prev["kst_free"][gb]])
                dkst_list.append(dkst)
                for q4 in range(4):
                    pqi = (g * 4 + q4) % 2
                    pqb = Pq[pqi]
                    for hh in range(2):
                        mm1 = p.ins("tensor", lambda e, q4=q4, gb=gb, hh=hh: e.matmul(
                            out=ps_sc[:, hh * 512:(hh + 1) * 512], lhsT=kst[gb][0:32, :],
                            rhs=QD[gb][0:32, q4 * 1024 + hh * 512:q4 * 1024 + (hh + 1) * 512],
                            start=True, stop=True),
                            deps=[dqd, dkst, prev["sc_free"]] if hh == 0 else [])
                    exq = p.ins("scalar", lambda e, pqb=pqb: e.activation(
                        out=pqb[:], in_=ps_sc[:], func=AF.Exp, scale=INVSQ128),
                        deps=[mm1, prev["pq_free"][pqi]])
                    prev["sc_free"] = exq
                    if upto < 4:
                        continue
                    t0 = g * 32 + q4 * 8
                    mmoz = None
                    for tl in range(8):
                        tok = t0 + tl
                        mmoz = p.ins("tensor", lambda e, pqb=pqb, tl=tl, tok=tok: e.matmul(
                            out=ps_oz[:, 2 * tok:2 * tok + 2],
                            lhsT=pqb[:, tl * 128:(tl + 1) * 128],
                            rhs=v_aug[:, 2 * tok:2 * tok + 2],
                            start=True, stop=True),
                            deps=[exq, va, prev["oz_free"]] if tl == 0 else [])
                    pv_extracts.append(mmoz)
                    prev["pq_free"][pqi] = mmoz
                prev["qd_free"][gb] = p.last("tensor")
                prev["kst_free"][gb] = p.last("tensor")
            if upto < 4:
                st_x1 = p.dma("scalar", lambda e, it=it: e.dma_start(
                    out=fm_slice_hbm(x1T, it * 128, 128),
                    in_=x1out[:].rearrange("p (c t) -> p c t", t=128)),
                    deps=[prev["sc_free"], p.last("scalar")])
                prev["x1out_st"] = st_x1
                prev["qkvbf_free"] = [tr2, va]
                continue
            pstep_oz = ps_oz[:].ap[0][0]
            oz_even = bass.AP(ps_oz, 0, [[pstep_oz, 128], [2, 128]])
            oz_odd = bass.AP(ps_oz, 1, [[pstep_oz, 128], [2, 128]])
            rc = p.ins("vector", lambda e: e.reciprocal(out=zr[:], in_=oz_odd),
                       deps=pv_extracts)
            cpo = p.ins("vector", lambda e: e.tensor_tensor(
                out=o_fm[:], in0=oz_even, in1=zr[:], op=ALU.mult),
                deps=[rc, p.last("tensor")])
            if upto < 5:
                st_x1 = p.dma("scalar", lambda e, it=it: e.dma_start(
                    out=fm_slice_hbm(x1T, it * 128, 128),
                    in_=x1out[:].rearrange("p (c t) -> p c t", t=128)),
                    deps=[cpo, p.last("scalar")])
                prev["x1out_st"] = st_x1
                prev["oz_free"] = cpo
                prev["qkvbf_free"] = [tr2, va]
                continue
            ev = None
            for half in range(8):
                hmm = None
                for cc in range(4):
                    c = half * 4 + cc
                    hmm = p.ins("tensor", lambda e, c=c, cc=cc, half=half: e.matmul(
                        out=ps_rb[:, (half % 2) * 512 + cc * 128:(half % 2) * 512 + (cc + 1) * 128],
                        lhsT=repl_s[:, c * 128:(c + 1) * 128], rhs=o_fm[:],
                        start=True, stop=True),
                        deps=[cpo, ld_w[3], ev] if cc == 0 else [])
                ev = p.ins("vector", lambda e, half=half, bi=bi: e.tensor_tensor(
                    out=x1pre[:, half * 512:(half + 1) * 512],
                    in0=ps_rb[:, (half % 2) * 512:(half % 2) * 512 + 512],
                    in1=xt_f[bi][:, half * 512:(half + 1) * 512], op=ALU.add),
                    deps=[hmm, ld_x, prev["ln_prev"]])
            x1pre_done = ev
            if upto < 6:
                st_x1 = p.dma("scalar", lambda e, it=it: e.dma_start(
                    out=fm_slice_hbm(x1T, it * 128, 128),
                    in_=x1pre[:].rearrange("p (c t) -> p c t", t=128)),
                    deps=[x1pre_done, p.last("scalar")])
                prev["x1out_st"] = st_x1
                prev["oz_free"] = cpo
                prev["qkvbf_free"] = [tr2, va]
                continue
            apply_dep, _ = emit_ln(p, x1pre, x1out, x1bf, xsq, ones_bf, ones_f, rows,
                                   ps_oz[:, 256:512], ps_rb, 128,
                                   [x1pre_done, prev["x1out_st"], p.last("tensor")],
                                   ln_upto=(upto - 5), bc_sb=bc_sb)
            st_x1 = p.dma("scalar", lambda e, it=it: e.dma_start(
                out=fm_slice_hbm(x1T, it * 128, 128),
                in_=x1out[:].rearrange("p (c t) -> p c t", t=128)), deps=[apply_dep])
            prev["x1out_st"] = st_x1
            prev["oz_free"] = apply_dep
            prev["ln_prev"] = [apply_dep, emit_ln.cast2]
            prev["xt_free"][bi] = apply_dep
            prev["dqd_list"] = dqd_list
            prev["dkst_list"] = dkst_list
            prev["vaug_free"] = p.last("tensor")
            prev["pstr_free"] = cp2
            prev["qkvbf_free"] = [tr2, va]

        p.seal()
        p.build(block)
    return nc




def build_l0ffn(T_total=TPC, upto=9, NXTF=2, tfrom=99, tupto=9):
    nc = bass.Bass("TRN2", target_bir_lowering=False, debug=False,
                   detect_race_conditions=RACE[0])
    x1T = nc.dram_tensor("x1T", [DM, T_total], F32, kind="ExternalInput")
    routew = nc.dram_tensor("routew", [8, DM], F32, kind="ExternalInput")
    aflat = nc.dram_tensor("aflat", [128, DM], F32, kind="ExternalInput")
    bw = nc.dram_tensor("bw", [8, DM, 16], F32, kind="ExternalInput")
    sexp = nc.dram_tensor("sexp", [8, 128], F32, kind="ExternalInput")
    onesf = nc.dram_tensor("onesf", [128, 128], F32, kind="ExternalInput")
    identb = nc.dram_tensor("identb", [128, 128], BF16, kind="ExternalInput")
    identf = nc.dram_tensor("identf", [128, 128], F32, kind="ExternalInput")
    x2T = nc.dram_tensor("x2T", [DM, T_total], F32, kind="ExternalOutput")

    NT = T_total // 128
    with ExitStack() as stack:
        E = stack.enter_context
        routew_s = E(nc.sbuf_tensor("routew_s", [8, DM], F32))
        routeT_s = E(nc.sbuf_tensor("routeT_s", [128, 256], F32))
        aflat_s = E(nc.sbuf_tensor("aflat_s", [128, DM], F32))
        aflat_b = E(nc.sbuf_tensor("aflat_b", [128, DM], BF16))
        afm_b = E(nc.sbuf_tensor("afm_b", [128, NFC * 128], BF16))
        bwst = E(nc.sbuf_tensor("bwst", [128, 128], F32))
        bwst_b = E(nc.sbuf_tensor("bwst_b", [128, 128], BF16))
        bwflat = E(nc.sbuf_tensor("bwflat", [128, NFC * 128], BF16))
        sexp_s = E(nc.sbuf_tensor("sexp_s", [8, 128], F32))
        ones_f = E(nc.sbuf_tensor("ones_f", [128, 128], F32))
        ones_bf = E(nc.sbuf_tensor("ones_bf", [128, 1], BF16))
        ident_b = E(nc.sbuf_tensor("ident_b", [128, 128], BF16))
        ident_f = E(nc.sbuf_tensor("ident_f", [128, 128], F32))
        xt_f = [E(nc.sbuf_tensor("xt_f%d" % i, [128, 4096], F32)) for i in range(NXTF)]
        xt_bf = E(nc.sbuf_tensor("xt_bf", [128, 4096], BF16))
        lg_tm = E(nc.sbuf_tensor("lg_tm", [128, 8], F32))
        m_col = E(nc.sbuf_tensor("m_col", [128, 1], F32))
        eq_c = E(nc.sbuf_tensor("eq_c", [128, 8], F32))
        first_c = E(nc.sbuf_tensor("first_c", [128, 8], F32))
        nf_c2 = E(nc.sbuf_tensor("nf_c2", [128, 8], F32))
        rw_tm = E(nc.sbuf_tensor("rw_tm", [128, 8], F32))
        rw_fm = E(nc.sbuf_tensor("rw_fm", [8, 128], F32))
        m16_s = E(nc.sbuf_tensor("m16_s", [128, 128], F32))
        wsel_s = E(nc.sbuf_tensor("wsel_s", [128, 128], BF16))
        x2pre = E(nc.sbuf_tensor("x2pre", [128, 4096], F32))
        x2bf = E(nc.sbuf_tensor("x2bf", [128, 4096], BF16))
        xsq = E(nc.sbuf_tensor("xsq", [128, 4096], BF16))
        x2out = E(nc.sbuf_tensor("x2out", [128, 4096], F32))
        rows = tuple(E(nc.sbuf_tensor("row%d" % i, [1, 512], F32)) for i in range(4))
        bc_sb = E(nc.sbuf_tensor("bc_sb", [128, 256], F32))
        ps_log = E(nc.psum_tensor("ps_log", [8, 128], F32))
        ps_sm = E(nc.psum_tensor("ps_sm", [128, 136], F32))
        ps_m16 = E(nc.psum_tensor("ps_m16", [128, 128], F32))
        ps_hs = E(nc.psum_tensor("ps_hs", [128, 128], F32))
        ps_o = [E(nc.psum_tensor("ps_o%d" % i, [128, 512], F32)) for i in range(2)]
        ps_bc = E(nc.psum_tensor("ps_bc", [128, 512], F32))
        ps_mb = E(nc.psum_tensor("ps_mb", [128, 128], BF16))
        ps_st = ps_bc
        p = Prog(nc, stack)
        block = E(nc.Block())

        ld = [p.dma("sync", lambda e: e.dma_start(out=routew_s[:], in_=routew.ap())),
              p.dma("sync", lambda e: e.dma_start(out=aflat_s[:], in_=aflat.ap())),
              p.dma("sync", lambda e: e.dma_start(out=sexp_s[:], in_=sexp.ap())),
              p.dma("sync", lambda e: e.dma_start(out=ones_f[:], in_=onesf.ap())),
              p.dma("sync", lambda e: e.dma_start(out=ident_b[:], in_=identb.ap())),
              p.dma("sync", lambda e: e.dma_start(out=ident_f[:], in_=identf.ap()))]
        onebf = p.ins("vector", lambda e: e.tensor_copy(out=ones_bf[:], in_=ones_f[:, 0:1]),
                      deps=[ld[3]])
        acast = p.ins("vector", lambda e: e.tensor_copy(out=aflat_b[:], in_=aflat_s[:]),
                      deps=[ld[1]])
        prev_cp = None
        for c in range(NFC):
            t1 = p.ins("tensor", lambda e, c=c: e.transpose(
                out=ps_sm[:, 0:8], in_=routew_s[:, c * 128:(c + 1) * 128],
                identity=ident_f[0:8, 0:8]), deps=[ld[0], ld[5], prev_cp])
            cp1 = p.ins("vector", lambda e, c=c: e.tensor_copy(
                out=routeT_s[:, c * 8:(c + 1) * 8], in_=ps_sm[:, 0:8]), deps=[t1])
            t2 = p.ins("tensor", lambda e, c=c: e.transpose(
                out=ps_mb[:], in_=aflat_b[:, c * 128:(c + 1) * 128],
                identity=ident_b[:]), deps=[acast, ld[4], prev_cp])
            cp2 = p.ins("vector", lambda e, c=c: e.tensor_copy(
                out=afm_b[:, c * 128:(c + 1) * 128], in_=ps_mb[:]), deps=[t2])
            prev_cp = [cp1, cp2]
        prev_bw = prev_cp
        for c in range(NFC):
            dms = []
            for ee in range(8):
                dms.append(p.dma("sync", lambda e, c=c, ee=ee: e.dma_start(
                    out=bwst[:, ee * 16:(ee + 1) * 16],
                    in_=bw.ap()[ee, c * 128:(c + 1) * 128, :]), deps=[prev_bw]))
            bcast = p.ins("vector", lambda e: e.tensor_copy(out=bwst_b[:], in_=bwst[:]),
                          deps=dms)
            tb1 = p.ins("tensor", lambda e: e.transpose(
                out=ps_mb[0:64, :], in_=bwst_b[:, 0:64], identity=ident_b[:]),
                deps=[bcast, prev_bw])
            cb1 = p.ins("vector", lambda e, c=c: e.tensor_copy(
                out=bwflat[0:64, c * 128:(c + 1) * 128], in_=ps_mb[0:64, :]), deps=[tb1])
            tb2 = p.ins("tensor", lambda e: e.transpose(
                out=ps_mb[0:64, :], in_=bwst_b[:, 64:128], identity=ident_b[:]),
                deps=[bcast, cb1])
            cb2 = p.ins("vector", lambda e, c=c: e.tensor_copy(
                out=bwflat[64:128, c * 128:(c + 1) * 128], in_=ps_mb[0:64, :]), deps=[tb2])
            prev_bw = [cb2]

        prev = {"x2out_st": None, "xt_free": [None] * NXTF, "log_free": None,
                "hs_free": None, "o_free": [None, None], "ln_prev": None,
                "m16_free": None, "wsel_free": None, "oz_free": None,
                "tr8_free": prev_cp, "mb_free": prev_bw}

        uptoG = upto
        for it in range(NT):
            upto = uptoG if it < tfrom else tupto
            bi = it % NXTF
            ld_x = p.dma("sync", lambda e, bi=bi, it=it: e.dma_start(
                out=xt_f[bi][:].rearrange("p (c t) -> p c t", t=128),
                in_=fm_slice_hbm(x1T, it * 128, 128)),
                deps=[prev["xt_free"][bi]])
            cast_x = p.ins("vector", lambda e, bi=bi: e.tensor_copy(
                out=xt_bf[:], in_=xt_f[bi][:]), deps=[ld_x, p.last("tensor")])
            if upto < 1:
                st_x2 = p.dma("scalar", lambda e, it=it: e.dma_start(
                    out=fm_slice_hbm(x2T, it * 128, 128),
                    in_=x2out[:].rearrange("p (c t) -> p c t", t=128)),
                    deps=[cast_x, p.last("scalar")])
                prev["x2out_st"] = st_x2
                prev["xt_free"][bi] = cast_x
                continue
            mmlg = None
            for c in range(NFC):
                mmlg = p.ins("tensor", lambda e, c=c, bi=bi: e.matmul(
                    out=ps_log[:], lhsT=routeT_s[:, c * 8:(c + 1) * 8],
                    rhs=xt_f[bi][:, c * 128:(c + 1) * 128],
                    start=(c == 0), stop=(c == NFC - 1)),
                    deps=[ld_x, prev["log_free"], prev_cp] if c == 0 else [])
            trlg = p.ins("tensor", lambda e: e.transpose(
                out=ps_sm[:, 0:8], in_=ps_log[:] if False else lg_tm[:, 0:8],
                identity=ident_f[:]), deps=[]) if False else None
            evlg = p.ins("vector", lambda e: e.tensor_copy(
                out=rw_fm[:], in_=ps_log[:]), deps=[mmlg])
            trlg = p.ins("tensor", lambda e: e.transpose(
                out=ps_sm[:, 0:8], in_=rw_fm[:], identity=ident_f[0:8, 0:8]),
                deps=[evlg, prev["tr8_free"]])
            cplg = p.ins("vector", lambda e: e.tensor_copy(out=lg_tm[:], in_=ps_sm[:, 0:8]),
                         deps=[trlg])
            if upto < 2:
                st_x2 = p.dma("scalar", lambda e, it=it: e.dma_start(
                    out=fm_slice_hbm(x2T, it * 128, 128),
                    in_=x2out[:].rearrange("p (c t) -> p c t", t=128)),
                    deps=[cplg, p.last("scalar")])
                prev["x2out_st"] = st_x2
                prev["xt_free"][bi] = cplg
                prev["log_free"] = cplg
                prev["tr8_free"] = cplg
                continue
            mx = p.ins("vector", lambda e: e.reduce_max(out=m_col[:], in_=lg_tm[:],
                                                        axis=AXX), deps=[cplg])
            eqa = p.ins("vector", lambda e: e.tensor_scalar(
                out=eq_c[:], in0=lg_tm[:], scalar1=m_col[:, 0:1], scalar2=None,
                op0=ALU.subtract), deps=[mx])
            eq = p.ins("vector", lambda e: e.tensor_single_scalar(
                out=eq_c[:], in_=eq_c[:], scalar=0.0, op=ALU.is_equal), deps=[eqa])
            sc1 = p.ins("vector", lambda e: e.tensor_tensor_scan(
                out=nf_c2[:], data0=eq_c[:], data1=ones_f[:, 0:8], initial=0.0,
                op0=ALU.add, op1=ALU.mult), deps=[eq, onebf])
            sc2 = p.ins("vector", lambda e: e.tensor_single_scalar(
                out=nf_c2[:], in_=nf_c2[:], scalar=1.0, op=ALU.is_equal), deps=[sc1])
            fdone = p.ins("vector", lambda e: e.tensor_tensor(
                out=first_c[:], in0=eq_c[:], in1=nf_c2[:], op=ALU.mult), deps=[sc2])
            rw = p.ins("vector", lambda e: e.tensor_scalar(
                out=rw_tm[:], in0=first_c[:], scalar1=m_col[:, 0:1], scalar2=None,
                op0=ALU.mult), deps=[fdone, prev.get("rwtm_free")])
            if upto < 3:
                st_x2 = p.dma("scalar", lambda e, it=it: e.dma_start(
                    out=fm_slice_hbm(x2T, it * 128, 128),
                    in_=x2out[:].rearrange("p (c t) -> p c t", t=128)),
                    deps=[rw, p.last("scalar")])
                prev["x2out_st"] = st_x2
                prev["xt_free"][bi] = rw
                prev["log_free"] = cplg
                prev["tr8_free"] = cplg
                continue
            trrw = p.ins("tensor", lambda e: e.transpose(
                out=ps_sm[:, 0:8], in_=rw_tm[:], identity=ident_f[:]), deps=[rw, cplg])
            cprw = p.ins("vector", lambda e: e.tensor_copy(
                out=rw_fm[:], in_=bass.AP(ps_tr8, 0, [[8, 128], [1, 8]])
                if False else ps_sm[:, 0:8]), deps=[trrw]) if False else None
            p.items["tensor"].pop()
            trrw = p.ins("tensor", lambda e: e.transpose(
                out=ps_sm[0:8, 8:136], in_=rw_tm[:], identity=ident_f[:]),
                deps=[rw, cplg, p.last("vector"), prev.get("rwT_free")])
            cprw = p.ins("vector", lambda e: e.tensor_copy(out=rw_fm[:], in_=ps_sm[0:8, 8:136]),
                         deps=[trrw])
            prev["rwT_free"] = cprw
            mm16 = p.ins("tensor", lambda e: e.matmul(
                out=ps_m16[:], lhsT=sexp_s[:], rhs=rw_fm[:], start=True, stop=True),
                deps=[cprw, ld[2], prev["m16_free"]])
            cp16 = p.ins("vector", lambda e: e.tensor_copy(out=m16_s[:], in_=ps_m16[:]),
                         deps=[mm16])
            prev["m16_free"] = cp16
            if upto < 4:
                st_x2 = p.dma("scalar", lambda e, it=it: e.dma_start(
                    out=fm_slice_hbm(x2T, it * 128, 128),
                    in_=x2out[:].rearrange("p (c t) -> p c t", t=128)),
                    deps=[cp16, p.last("scalar")])
                prev["x2out_st"] = st_x2
                prev["xt_free"][bi] = cp16
                prev["log_free"] = cprw
                prev["tr8_free"] = cplg
                continue
            mmhs = None
            for c in range(NFC):
                mmhs = p.ins("tensor", lambda e, c=c: e.matmul(
                    out=ps_hs[:], lhsT=afm_b[:, c * 128:(c + 1) * 128],
                    rhs=xt_bf[:, c * 128:(c + 1) * 128],
                    start=(c == 0), stop=(c == NFC - 1)),
                    deps=[cast_x, prev["hs_free"]] if c == 0 else [])
            wsel = p.ins("vector", lambda e: e.tensor_tensor(
                out=wsel_s[:], in0=ps_hs[:], in1=m16_s[:], op=ALU.mult),
                deps=[mmhs, cp16, prev["wsel_free"]])
            prev["hs_free"] = wsel
            if upto < 5:
                st_x2 = p.dma("scalar", lambda e, it=it: e.dma_start(
                    out=fm_slice_hbm(x2T, it * 128, 128),
                    in_=x2out[:].rearrange("p (c t) -> p c t", t=128)),
                    deps=[wsel, p.last("scalar")])
                prev["x2out_st"] = st_x2
                prev["xt_free"][bi] = wsel
                prev["log_free"] = cprw
                prev["tr8_free"] = cplg
                continue
            ev = None
            for grp in range(8):
                oi = grp % 2
                mo = None
                for cc in range(4):
                    c = grp * 4 + cc
                    mo = p.ins("tensor", lambda e, c=c, cc=cc, oi=oi: e.matmul(
                        out=ps_o[oi][:, cc * 128:(cc + 1) * 128],
                        lhsT=bwflat[:, c * 128:(c + 1) * 128], rhs=wsel_s[:],
                        start=True, stop=True),
                        deps=[wsel, prev["o_free"][oi]] if cc == 0 else [])
                ev = p.ins("vector", lambda e, grp=grp, oi=oi, bi=bi: e.tensor_tensor(
                    out=x2pre[:, grp * 512:(grp + 1) * 512],
                    in0=ps_o[oi][:],
                    in1=xt_f[bi][:, grp * 512:(grp + 1) * 512], op=ALU.add),
                    deps=[mo, ld_x, prev["ln_prev"]])
                prev["o_free"][oi] = ev
            prev["wsel_free"] = p.last("tensor")
            x2pre_done = ev
            if upto < 6:
                st_x2 = p.dma("scalar", lambda e, it=it: e.dma_start(
                    out=fm_slice_hbm(x2T, it * 128, 128),
                    in_=x2pre[:].rearrange("p (c t) -> p c t", t=128)),
                    deps=[x2pre_done, p.last("scalar")])
                prev["x2out_st"] = st_x2
                prev["xt_free"][bi] = x2pre_done
                prev["log_free"] = cprw
                prev["tr8_free"] = cplg
                continue
            apply_dep, _ = emit_ln(p, x2pre, x2out, x2bf, xsq, ones_bf, ones_f, rows,
                                   ps_bc[0:1, 256:512], ps_bc, 128,
                                   [x2pre_done, prev["x2out_st"], p.last("tensor")],
                                   bc_sb=bc_sb)
            st_x2 = p.dma("scalar", lambda e, it=it: e.dma_start(
                out=fm_slice_hbm(x2T, it * 128, 128),
                in_=x2out[:].rearrange("p (c t) -> p c t", t=128)), deps=[apply_dep])
            prev["x2out_st"] = st_x2
            prev["ln_prev"] = [apply_dep, emit_ln.cast2]
            prev["xt_free"][bi] = apply_dep
            prev["log_free"] = evlg
            prev["tr8_free"] = cplg
            prev["rwtm_free"] = trrw

        p.seal()
        p.build(block)
    return nc




def build_qkv(T_total=TPC):
    nc = bass.Bass("TRN2", target_bir_lowering=False, debug=False,
                   detect_race_conditions=RACE[0])
    x2T = nc.dram_tensor("x2T", [DM, T_total], F32, kind="ExternalInput")
    wq1 = nc.dram_tensor("wq1", [DM, 4096], BF16, kind="ExternalInput")
    wk1 = nc.dram_tensor("wk1", [DM, 1024], BF16, kind="ExternalInput")
    wv1 = nc.dram_tensor("wv1", [DM, 1024], BF16, kind="ExternalInput")
    bq1 = nc.dram_tensor("bq1", [128, 32], F32, kind="ExternalInput")
    bk1 = nc.dram_tensor("bk1", [128, 8], F32, kind="ExternalInput")
    bv1 = nc.dram_tensor("bv1", [128, 8], F32, kind="ExternalInput")
    invfr = nc.dram_tensor("invfr", [1, 128], F32, kind="ExternalInput")
    permb = nc.dram_tensor("permb", [128, 128], BF16, kind="ExternalInput")
    posr = nc.dram_tensor("posr", [1, T_total], F32, kind="ExternalInput")
    identb = nc.dram_tensor("identb", [128, 128], BF16, kind="ExternalInput")
    Qf = nc.dram_tensor("Qf", [4096, T_total], BF16, kind="ExternalOutput")
    Kf = nc.dram_tensor("Kf", [1024, T_total], BF16, kind="ExternalOutput")
    Vt = nc.dram_tensor("Vt", [8, T_total, 128], BF16, kind="ExternalOutput")

    NH = T_total // 512
    with ExitStack() as stack:
        E = stack.enter_context
        xtmp = [E(nc.sbuf_tensor("xtmp%d" % i, [128, 4096], F32)) for i in range(2)]
        x2b = E(nc.sbuf_tensor("x2b", [128, NFC * 512], BF16))
        wb = [E(nc.sbuf_tensor("wb%d" % i, [128, NFC * 128], BF16)) for i in range(2)]
        bq_s = E(nc.sbuf_tensor("bq_s", [128, 32], F32))
        bk_s = E(nc.sbuf_tensor("bk_s", [128, 8], F32))
        bv_s = E(nc.sbuf_tensor("bv_s", [128, 8], F32))
        invf_s = E(nc.sbuf_tensor("invf_s", [1, 128], F32))
        perm_s = E(nc.sbuf_tensor("perm_s", [128, 128], BF16))
        sign_b = E(nc.sbuf_tensor("sign_b", [128, 1], BF16))
        pos_s = E(nc.sbuf_tensor("pos_s", [1, 1024], F32))
        ident_b = E(nc.sbuf_tensor("ident_b", [128, 128], BF16))
        ang_s = E(nc.sbuf_tensor("ang_s", [128, 1024], F32))
        angi = E(nc.sbuf_tensor("angi", [128, 1024], I32))
        angf = E(nc.sbuf_tensor("angf", [128, 1024], F32))
        wrap = E(nc.sbuf_tensor("wrap", [128, 1024], F32))
        sc_b = E(nc.sbuf_tensor("sc_b", [128, 1024], BF16))
        sneg = E(nc.sbuf_tensor("sneg", [128, 512], BF16))
        t_bf = E(nc.sbuf_tensor("t_bf", [128, 512], BF16))
        rope_s = E(nc.sbuf_tensor("rope_s", [128, 512], BF16))
        out_bf = E(nc.sbuf_tensor("out_bf", [128, 512], BF16))
        vt_s = E(nc.sbuf_tensor("vt_s", [128, 512], BF16))
        ps_mm = [E(nc.psum_tensor("ps_mm%d" % i, [128, 512], F32)) for i in range(2)]
        ps_ang = E(nc.psum_tensor("ps_ang", [128, 512], F32))
        ps_sw = E(nc.psum_tensor("ps_sw", [128, 512], F32))
        ps_tr = E(nc.psum_tensor("ps_tr", [128, 128], BF16))
        p = Prog(nc, stack)
        block = E(nc.Block())

        ld0 = [p.dma("sync", lambda e: e.dma_start(out=bq_s[:], in_=bq1.ap())),
               p.dma("sync", lambda e: e.dma_start(out=bk_s[:], in_=bk1.ap())),
               p.dma("sync", lambda e: e.dma_start(out=bv_s[:], in_=bv1.ap())),
               p.dma("sync", lambda e: e.dma_start(out=invf_s[:], in_=invfr.ap())),
               p.dma("sync", lambda e: e.dma_start(out=pos_s[0:1, 0:T_total],
                                                   in_=posr.ap())),
               p.dma("sync", lambda e: e.dma_start(out=ident_b[:], in_=identb.ap())),
               p.dma("sync", lambda e: e.dma_start(out=perm_s[:], in_=permb.ap()))]
        sg1 = p.ins("vector", lambda e: e.memset(sign_b[0:64, :], -1.0))
        sg2 = p.ins("vector", lambda e: e.memset(sign_b[64:128, :], 1.0), deps=[sg1])

        prev = {"x2b_rd": None, "wb_rd": [None, None], "mm_rd": [None, None],
                "tbf_rd": None, "rope_rd": None, "out_rd": None, "vt_rd": None,
                "ang_rd": None, "xtmp_rd": [None, None], "psw_rd": None,
                "sneg_rd": None}

        for half in range(NH):
            t0 = half * 512
            casts = []
            for tb in range(4):
                xbuf = xtmp[tb % 2]
                ldx = p.dma("sync", lambda e, xbuf=xbuf, t0=t0, tb=tb: e.dma_start(
                    out=xbuf[:].rearrange("p (c t) -> p c t", t=128),
                    in_=fm_slice_hbm(x2T, t0 + tb * 128, 128)),
                    deps=[prev["xtmp_rd"][tb % 2]])
                cst = p.ins("vector", lambda e, xbuf=xbuf, tb=tb: e.tensor_copy(
                    out=x2b[:].rearrange("p (c t) -> p c t", t=512)[:, :, tb * 128:(tb + 1) * 128],
                    in_=xbuf[:].rearrange("p (c t) -> p c t", t=128)),
                    deps=[ldx, prev["x2b_rd"]])
                casts.append(cst)
                prev["xtmp_rd"][tb % 2] = cst
            mm_ang = p.ins("tensor", lambda e, t0=t0: e.matmul(
                out=ps_ang[:], lhsT=invf_s[:], rhs=pos_s[0:1, t0:t0 + 512],
                start=True, stop=True), deps=[ld0[3], ld0[4], prev["ang_rd"]])
            cpa = p.ins("vector", lambda e: e.tensor_copy(out=ang_s[:, 0:512],
                                                          in_=ps_ang[:]),
                        deps=[mm_ang])
            cpc = p.ins("vector", lambda e: e.tensor_single_scalar(
                out=ang_s[:, 512:1024], in_=ang_s[:, 0:512],
                scalar=float(np.pi / 2), op=ALU.add), deps=[cpa])
            u1 = p.ins("vector", lambda e: e.tensor_single_scalar(
                out=angf[:], in_=ang_s[:], scalar=float(1.0 / (2 * np.pi)),
                op=ALU.mult), deps=[cpc])
            u2 = p.ins("vector", lambda e: e.tensor_copy(out=angi[:], in_=angf[:]),
                       deps=[u1])
            u3 = p.ins("vector", lambda e: e.tensor_copy(out=angf[:], in_=angi[:]),
                       deps=[u2])
            u4 = p.ins("vector", lambda e: e.scalar_tensor_tensor(
                out=angf[:], in0=angf[:], scalar=float(-2 * np.pi), in1=ang_s[:],
                op0=ALU.mult, op1=ALU.add), deps=[u3])
            u5 = p.ins("vector", lambda e: e.tensor_single_scalar(
                out=wrap[:], in_=angf[:], scalar=float(np.pi), op=ALU.is_ge),
                deps=[u4])
            u6 = p.ins("vector", lambda e: e.scalar_tensor_tensor(
                out=angf[:], in0=wrap[:], scalar=float(-2 * np.pi), in1=angf[:],
                op0=ALU.mult, op1=ALU.add), deps=[u5])
            s1 = p.ins("scalar", lambda e: e.activation(
                out=sc_b[:], in_=angf[:], func=AF.Sin), deps=[u6, prev["rope_rd"]])
            s3 = p.ins("vector", lambda e: e.tensor_tensor(
                out=sneg[:], in0=sc_b[:, 0:512],
                in1=sign_b[:, 0:1].broadcast_to([128, 512]), op=ALU.mult),
                deps=[s1, sg2, prev["sneg_rd"]] if prev.get("sneg_rd") else [s1, sg2])
            tabs = [s1, s3]
            prev["ang_rd"] = u1

            for oc in range(48):
                wi = oc % 2
                if oc < 32:
                    wsrc, wcol, b_ap = wq1, oc, bq_s[:, oc:oc + 1]
                    bdep = ld0[0]
                elif oc < 40:
                    wsrc, wcol, b_ap = wk1, oc - 32, bk_s[:, oc - 32:oc - 31]
                    bdep = ld0[1]
                else:
                    wsrc, wcol, b_ap = wv1, oc - 40, bv_s[:, oc - 40:oc - 39]
                    bdep = ld0[2]
                ldw = p.dma("sync", lambda e, wi=wi, wsrc=wsrc, wcol=wcol: e.dma_start(
                    out=wb[wi][:].rearrange("p (c d) -> p c d", d=128),
                    in_=wsrc.ap()[:, wcol * 128:(wcol + 1) * 128]
                        .rearrange("(c p) o -> p c o", p=128)),
                    deps=[prev["wb_rd"][wi]])
                mm = None
                for c in range(NFC):
                    mm = p.ins("tensor", lambda e, wi=wi, c=c: e.matmul(
                        out=ps_mm[wi][:], lhsT=wb[wi][:, c * 128:(c + 1) * 128],
                        rhs=x2b[:, c * 512:(c + 1) * 512],
                        start=(c == 0), stop=(c == NFC - 1)),
                        deps=([ldw, prev["mm_rd"][wi]] + casts) if c == 0 else [])
                prev["wb_rd"][wi] = mm
                evq = p.ins("scalar", lambda e, wi=wi, b_ap=b_ap: e.activation(
                    out=t_bf[:], in_=ps_mm[wi][:], func=AF.Identity, bias=b_ap),
                    deps=[mm, bdep, prev["tbf_rd"]])
                prev["mm_rd"][wi] = evq
                if oc < 40:
                    msw = p.ins("tensor", lambda e: e.matmul(
                        out=ps_sw[:], lhsT=perm_s[:], rhs=t_bf[:],
                        start=True, stop=True),
                        deps=[evq, ld0[6], prev.get("psw_rd")])
                    r1 = p.ins("vector", lambda e: e.tensor_tensor(
                        out=out_bf[:], in0=t_bf[:], in1=sc_b[:, 512:1024],
                        op=ALU.mult), deps=[evq] + tabs + [prev["out_rd"]])
                    r2 = p.ins("vector", lambda e: e.tensor_tensor(
                        out=rope_s[:], in0=ps_sw[:], in1=sneg[:],
                        op=ALU.mult), deps=[msw, tabs[1], prev["rope_rd"]])
                    prev["psw_rd"] = r2
                    prev["tbf_rd"] = [r1, msw]
                    o2 = p.ins("vector", lambda e: e.tensor_tensor(
                        out=out_bf[:], in0=out_bf[:], in1=rope_s[:],
                        op=ALU.add), deps=[r1, r2])
                    prev["rope_rd"] = o2
                    dst = Qf if oc < 32 else Kf
                    dco = wcol
                    sto = p.dma("scalar", lambda e, dst=dst, dco=dco, t0=t0: e.dma_start(
                        out=dst.ap()[dco * 128:(dco + 1) * 128, t0:t0 + 512],
                        in_=out_bf[:]), deps=[o2])
                    prev["out_rd"] = sto
                else:
                    g = oc - 40
                    trs = []
                    for tb in range(4):
                        tr = p.ins("tensor", lambda e, tb=tb: e.transpose(
                            out=ps_tr[:], in_=t_bf[:, tb * 128:(tb + 1) * 128],
                            identity=ident_b[:]),
                            deps=[evq, ld0[5]] + ([prev["vt_rd"]] if tb == 0 else
                                                  [trs[-1]]))
                        cpv = p.ins("vector", lambda e, tb=tb: e.tensor_copy(
                            out=vt_s[:, tb * 128:(tb + 1) * 128], in_=ps_tr[:]),
                            deps=[tr])
                        trs.append(cpv)
                    prev["tbf_rd"] = p.last("tensor")
                    stv = p.dma("scalar", lambda e, g=g, t0=t0: e.dma_start(
                        out=Vt.ap()[g, t0:t0 + 512, :]
                            .rearrange("(tb p) d -> p tb d", p=128),
                        in_=vt_s[:].rearrange("p (tb d) -> p tb d", d=128)), deps=trs)
                    prev["vt_rd"] = stv
            prev["x2b_rd"] = p.last("tensor")
            prev["sneg_rd"] = p.last("vector")

        p.seal()
        p.build(block)
    return nc




def build_attn(S_len=S, NHD=16):
    nc = bass.Bass("TRN2", target_bir_lowering=False, debug=False,
                   detect_race_conditions=RACE[0])
    Qs = nc.dram_tensor("Qs", [NHD * 128, S_len], BF16, kind="ExternalInput")
    Ks = nc.dram_tensor("Ks", [NHD // 4 * 128, S_len], BF16, kind="ExternalInput")
    Vts = nc.dram_tensor("Vts", [NHD // 4, S_len, 128], BF16, kind="ExternalInput")
    identb = nc.dram_tensor("identb", [128, 128], BF16, kind="ExternalInput")
    maskc = nc.dram_tensor("maskc", [128, 128], BF16, kind="ExternalInput")
    onesf = nc.dram_tensor("onesf", [128, 128], F32, kind="ExternalInput")
    ctxT = nc.dram_tensor("ctxT", [NHD * 128, S_len], BF16, kind="ExternalOutput")

    NG = NHD // 4
    NB = S_len // 128
    with ExitStack() as stack:
        E = stack.enter_context
        k_s = E(nc.sbuf_tensor("k_s", [128, NG * S_len], BF16))
        v_s = E(nc.sbuf_tensor("v_s", [128, NG * S_len], BF16))
        q_s = [E(nc.sbuf_tensor("q_s%d" % i, [128, S_len], BF16)) for i in range(2)]
        pt_s = [E(nc.sbuf_tensor("pt_s%d" % i, [128, 512], BF16)) for i in range(2)]
        ctxh = [E(nc.sbuf_tensor("ctxh%d" % i, [128, S_len], BF16)) for i in range(2)]
        zrec = E(nc.sbuf_tensor("zrec", [1, 512], F32))
        zb_s = E(nc.sbuf_tensor("zb_s", [128, 128], F32))
        ident_b = E(nc.sbuf_tensor("ident_b", [128, 128], BF16))
        mask_s = E(nc.sbuf_tensor("mask_s", [128, 128], BF16))
        ones_f = E(nc.sbuf_tensor("ones_f", [128, 128], F32))
        ones_bf = E(nc.sbuf_tensor("ones_bf", [128, 1], BF16))
        ps_s = [E(nc.psum_tensor("ps_s%d" % i, [128, 512], F32)) for i in range(2)]
        ps_c = E(nc.psum_tensor("ps_c", [128, 512], F32))
        ps_zf = E(nc.psum_tensor("ps_zf", [128, 512], F32))
        ps_z = ps_zf
        ps_zbf = E(nc.psum_tensor("ps_zbf", [128, 512], F32))
        p = Prog(nc, stack)
        block = E(nc.Block())

        ld0 = [p.dma("sync", lambda e: e.dma_start(out=ident_b[:], in_=identb.ap())),
               p.dma("sync", lambda e: e.dma_start(out=mask_s[:], in_=maskc.ap())),
               p.dma("sync", lambda e: e.dma_start(out=ones_f[:], in_=onesf.ap())),
               p.dma("sync", lambda e: e.dma_start(
                   out=k_s[:].rearrange("p (g t) -> p g t", g=NG),
                   in_=Ks.ap().rearrange("(g p) t -> p g t", p=128)))]
        for g in range(NG):
            ld0.append(p.dma("sync", lambda e, g=g: e.dma_start(
                out=v_s[:].rearrange("p (g tb d) -> p g tb d", g=NG, d=128)[:, g],
                in_=Vts.ap()[g].rearrange("(tb p) d -> p tb d", p=128))))
        onebf = p.ins("vector", lambda e: e.tensor_copy(out=ones_bf[:], in_=ones_f[:, 0:1]),
                      deps=[ld0[2]])

        prev = {"q_rd": [None, None], "pt_rd": [None, None], "ctx_st": [None, None],
                "zrec_rd": None, "zb_rd": None, "sc_free": [None, None],
                "ctx_rd": [None, None]}

        for h in range(NHD):
            g = h // 4
            qi = h % 2
            ldq = p.dma("sync", lambda e, h=h, qi=qi: e.dma_start(
                out=q_s[qi][:], in_=Qs.ap()[h * 128:(h + 1) * 128, :]),
                deps=[prev["q_rd"][qi]])
            for qb in range(NB):
                ci = qb % 2
                nkb = qb + 1
                nbat = (nkb + 3) // 4
                zmm = None
                for bat in range(nbat):
                    si = bat % 2
                    kb0 = bat * 4
                    kbn = min(4, nkb - kb0)
                    mm = None
                    for kk in range(kbn):
                        kb = kb0 + kk
                        is_diag = (kb == qb)
                        mm = p.ins("tensor", lambda e, g=g, kb=kb, kk=kk, si=si,
                                   qb=qb, qi=qi, stop_=not is_diag: e.matmul(
                            out=ps_s[si][:, kk * 128:(kk + 1) * 128],
                            lhsT=k_s[:, g * S_len + kb * 128:g * S_len + (kb + 1) * 128],
                            rhs=q_s[qi][:, qb * 128:(qb + 1) * 128],
                            start=True, stop=stop_),
                            deps=[ldq, ld0[3], prev["sc_free"][si]] if kk == 0
                            else [])
                        if is_diag:
                            mm = p.ins("tensor", lambda e, kk=kk, si=si: e.matmul(
                                out=ps_s[si][:, kk * 128:(kk + 1) * 128],
                                lhsT=mask_s[:], rhs=ident_b[:],
                                start=False, stop=True), deps=[ld0[0], ld0[1]])
                    pti = bat % 2
                    exq = p.ins("scalar", lambda e, si=si, pti=pti, w=kbn * 128: e.activation(
                        out=pt_s[pti][:, 0:w], in_=ps_s[si][:, 0:w], func=AF.Exp,
                        scale=INVSQ128), deps=[mm, prev["pt_rd"][pti]])
                    prev["sc_free"][si] = exq
                    cmm = None
                    for kk in range(kbn):
                        kb = kb0 + kk
                        cmm = p.ins("tensor", lambda e, g=g, kb=kb, kk=kk, pti=pti,
                                    ci=ci, first=(kb == 0), last=(kb == nkb - 1):
                                    e.matmul(
                            out=ps_c[:, ci * 128:(ci + 1) * 128],
                            lhsT=v_s[:, (g * NB + kb) * 128:(g * NB + kb + 1) * 128],
                            rhs=pt_s[pti][:, kk * 128:(kk + 1) * 128],
                            start=first, stop=last),
                            deps=[exq] + ([prev["ctx_rd"][ci]] if kb == 0 else []))
                    zmm = p.ins("tensor", lambda e, pti=pti, w=kbn * 128,
                                first=(bat == 0), last=(bat == nbat - 1): e.matmul(
                        out=ps_zf[0:1, 0:w], lhsT=ones_bf[:, 0:1],
                        rhs=pt_s[pti][:, 0:w],
                        start=first, stop=last),
                        deps=[exq, onebf] + ([prev["zrec_rd"]] if bat == 0 else []))
                    prev["pt_rd"][pti] = [cmm, zmm]
                nkbl = min(nkb, 4)
                zf = p.ins("vector", lambda e, nkbl=nkbl: e.reduce_sum(
                    out=zrec[0:1, 0:128],
                    in_=bass.AP(ps_zf, 0, [[512, 1], [1, 128], [128, nkbl]]),
                    axis=AXX), deps=[zmm])
                zr = p.ins("vector", lambda e: e.reciprocal(
                    out=zrec[0:1, 128:256], in_=zrec[0:1, 0:128]), deps=[zf])
                prev["zrec_rd"] = zr
                zbm = p.ins("tensor", lambda e: e.matmul(
                    out=ps_zbf[:, 0:128], lhsT=ones_f[0:1, :], rhs=zrec[0:1, 128:256],
                    start=True, stop=True), deps=[zr, prev["zb_rd"]])
                zbc = p.ins("vector", lambda e: e.tensor_copy(out=zb_s[:],
                                                              in_=ps_zbf[:, 0:128]),
                            deps=[zbm])
                prev["zb_rd"] = zbc
                evc = p.ins("vector", lambda e, ci=ci, qb=qb, qi=qi: e.tensor_tensor(
                    out=ctxh[qi][:, qb * 128:(qb + 1) * 128],
                    in0=ps_c[:, ci * 128:(ci + 1) * 128], in1=zb_s[:], op=ALU.mult),
                    deps=[cmm, zbc, prev["ctx_st"][qi]] if qb < 2 else [cmm, zbc])
                prev["ctx_rd"][ci] = evc
            stc = p.dma("scalar", lambda e, h=h, qi=qi: e.dma_start(
                out=ctxT.ap()[h * 128:(h + 1) * 128, :], in_=ctxh[qi][:]),
                deps=[p.last("vector")])
            prev["ctx_st"][qi] = stc
            prev["q_rd"][qi] = p.last("tensor")

        p.seal()
        p.build(block)
    return nc




def build_wo_ln3(T_total=TPC):
    nc = bass.Bass("TRN2", target_bir_lowering=False, debug=False,
                   detect_race_conditions=RACE[0])
    ctxTs = nc.dram_tensor("ctxTs", [DM, T_total], BF16, kind="ExternalInput")
    wo1 = nc.dram_tensor("wo1", [DM, 4096], BF16, kind="ExternalInput")
    bo1 = nc.dram_tensor("bo1", [128, 32], F32, kind="ExternalInput")
    x2T = nc.dram_tensor("x2T", [DM, T_total], F32, kind="ExternalInput")
    onesf = nc.dram_tensor("onesf", [128, 128], F32, kind="ExternalInput")
    x3T = nc.dram_tensor("x3T", [DM, T_total], F32, kind="ExternalOutput")
    urow = nc.dram_tensor("urow", [1, T_total], F32, kind="ExternalOutput")

    TQ = 256
    NQ = T_total // TQ
    with ExitStack() as stack:
        E = stack.enter_context
        ctx_b = E(nc.sbuf_tensor("ctx_b", [128, NFC * TQ], BF16))
        x2h = E(nc.sbuf_tensor("x2h", [128, NFC * TQ], F32))
        wb = [E(nc.sbuf_tensor("wb%d" % i, [128, NFC * 128], BF16)) for i in range(2)]
        bo_s = E(nc.sbuf_tensor("bo_s", [128, 32], F32))
        ones_f = E(nc.sbuf_tensor("ones_f", [128, 128], F32))
        ones_bf = E(nc.sbuf_tensor("ones_bf", [128, 1], BF16))
        x3pre = E(nc.sbuf_tensor("x3pre", [128, NFC * TQ], F32))
        x3o = E(nc.sbuf_tensor("x3o", [128, NFC * TQ], F32))
        xbf = E(nc.sbuf_tensor("xbf", [128, NFC * TQ], BF16))
        xsq = E(nc.sbuf_tensor("xsq", [128, NFC * TQ], BF16))
        rows = tuple(E(nc.sbuf_tensor("row%d" % i, [1, 512], F32)) for i in range(4))
        bc_sb = E(nc.sbuf_tensor("bc_sb", [128, 512], F32))
        u_all = E(nc.sbuf_tensor("u_all", [1, 1024], F32))
        ps_mm = [E(nc.psum_tensor("ps_mmE%d" % i, [128, 512], F32)) for i in range(2)]
        ps_st = E(nc.psum_tensor("ps_st", [128, 512], F32))
        ps_bc = E(nc.psum_tensor("ps_bc", [128, 512], F32))
        p = Prog(nc, stack)
        block = E(nc.Block())

        ld0 = [p.dma("sync", lambda e: e.dma_start(out=bo_s[:], in_=bo1.ap())),
               p.dma("sync", lambda e: e.dma_start(out=ones_f[:], in_=onesf.ap()))]
        onebf = p.ins("vector", lambda e: e.tensor_copy(out=ones_bf[:], in_=ones_f[:, 0:1]),
                      deps=[ld0[1]])

        prev = {"ctx_rd": None, "x2_rd": None, "wb_rd": [None, None],
                "mm_rd": [None, None], "x3_st": None, "ln_prev": None,
                "u_st": None}

        for q in range(NQ):
            t0 = q * TQ
            ldc = p.dma("sync", lambda e, t0=t0: e.dma_start(
                out=ctx_b[:].rearrange("p (c t) -> p c t", t=TQ),
                in_=fm_slice_hbm(ctxTs, t0, TQ)), deps=[prev["ctx_rd"]])
            ldx = p.dma("sync", lambda e, t0=t0: e.dma_start(
                out=x2h[:].rearrange("p (c t) -> p c t", t=TQ),
                in_=fm_slice_hbm(x2T, t0, TQ)), deps=[prev["x2_rd"]])
            ev = None
            for oc in range(NFC):
                wi = oc % 2
                ldw = p.dma("sync", lambda e, wi=wi, oc=oc: e.dma_start(
                    out=wb[wi][:].rearrange("p (c d) -> p c d", d=128),
                    in_=wo1.ap()[:, oc * 128:(oc + 1) * 128]
                        .rearrange("(c p) o -> p c o", p=128)),
                    deps=[prev["wb_rd"][wi]])
                mm = None
                for c in range(NFC):
                    mm = p.ins("tensor", lambda e, wi=wi, c=c: e.matmul(
                        out=ps_mm[wi][:, 0:256],
                        lhsT=wb[wi][:, c * 128:(c + 1) * 128],
                        rhs=ctx_b[:, c * TQ:(c + 1) * TQ],
                        start=(c == 0), stop=(c == NFC - 1)),
                        deps=[ldw, ldc, prev["mm_rd"][wi]] if c == 0 else [])
                prev["wb_rd"][wi] = mm
                ev = p.ins("vector", lambda e, wi=wi, oc=oc: e.scalar_tensor_tensor(
                    out=x3pre[:, oc * TQ:(oc + 1) * TQ],
                    in0=ps_mm[wi][:, 0:256],
                    scalar=bo_s[:, oc:oc + 1], in1=x2h[:, oc * TQ:(oc + 1) * TQ],
                    op0=ALU.add, op1=ALU.add),
                    deps=[mm, ldx, ld0[0], prev["ln_prev"]])
                prev["mm_rd"][wi] = ev
            apply_dep, dep_u = emit_ln(p, x3pre, x3o, xbf, xsq, ones_bf, ones_f, rows,
                                       ps_st, ps_bc, TQ,
                                       [ev, prev["x3_st"], p.last("tensor")],
                                       want_u=True, bc_sb=bc_sb)
            ucp = p.ins("vector", lambda e, q=q: e.tensor_copy(
                out=u_all[0:1, q * TQ:(q + 1) * TQ], in_=rows[3][0:1, 0:TQ]),
                deps=[dep_u, prev["u_st"]])
            st3 = p.dma("scalar", lambda e, t0=t0: e.dma_start(
                out=fm_slice_hbm(x3T, t0, TQ),
                in_=x3o[:].rearrange("p (c t) -> p c t", t=TQ)), deps=[apply_dep])
            prev["x3_st"] = st3
            prev["ln_prev"] = [apply_dep, emit_ln.cast2, ucp]
            prev["ctx_rd"] = p.last("tensor")
            prev["x2_rd"] = apply_dep
        stu = p.dma("scalar", lambda e: e.dma_start(out=urow.ap(), in_=u_all[0:1, :]),
                    deps=[p.last("vector")])
        p.seal()
        p.build(block)
    return nc


def build_dequant(nrows=512):
    nc = bass.Bass("TRN2", target_bir_lowering=False, debug=False,
                   detect_race_conditions=RACE[0])
    qw = nc.dram_tensor("qw", [8, nrows, 128], I32, kind="ExternalInput")
    lut4s = nc.dram_tensor("lut4s", [nrows, 16], F32, kind="ExternalInput")
    lut8s = nc.dram_tensor("lut8s", [nrows, 256], F32, kind="ExternalInput")
    identb = nc.dram_tensor("identb", [128, 128], BF16, kind="ExternalInput")
    p2t_h = nc.dram_tensor("p2t", [128, 1024], F32, kind="ExternalInput")
    WT4 = nc.dram_tensor("WT4", [DM, nrows], BF16, kind="ExternalOutput")
    WT8 = nc.dram_tensor("WT8", [DM, nrows], BF16, kind="ExternalOutput")


    NOC = nrows // 128
    with ExitStack() as stack:
        E = stack.enter_context
        q_all = E(nc.sbuf_tensor("q_all", [128, 1024], I32))
        xsh = E(nc.sbuf_tensor("xsh", [128, 1024], I32))
        bits = E(nc.sbuf_tensor("bits", [128, 1024], F32))
        wgt = E(nc.sbuf_tensor("wgt", [128, 1024], F32))
        code = E(nc.sbuf_tensor("code", [128, 4096], F32))
        code_b = E(nc.sbuf_tensor("code_b", [128, 4096], BF16))
        lo_b = E(nc.sbuf_tensor("lo_b", [128, 4096], BF16))
        tmpf = E(nc.sbuf_tensor("tmpf", [128, 4096], F32))
        tmpi = E(nc.sbuf_tensor("tmpi", [128, 4096], I32))
        eq_b = E(nc.sbuf_tensor("eq_b", [128, 4096], BF16))
        acc4 = E(nc.sbuf_tensor("acc4", [128, 4096], BF16))
        acc8 = E(nc.sbuf_tensor("acc8", [128, 4096], BF16))
        lut4c = E(nc.sbuf_tensor("lut4c", [128, 16], F32))
        lut8c = E(nc.sbuf_tensor("lut8c", [128, 256], F32))
        p2t = E(nc.sbuf_tensor("p2ts", [128, 1024], F32))
        ident_b = E(nc.sbuf_tensor("ident_b", [128, 128], BF16))
        wtst = E(nc.sbuf_tensor("wtst", [128, 512], BF16))
        ps_tr = [E(nc.psum_tensor("ps_trF%d" % i, [128, 1024], BF16)) for i in range(2)]
        p = Prog(nc, stack)
        block = E(nc.Block())

        ld0 = [p.dma("sync", lambda e: e.dma_start(out=ident_b[:], in_=identb.ap())),
               p.dma("sync", lambda e: e.dma_start(out=p2t[:], in_=p2t_h.ap()))]

        prev = {"q_rd": None, "lut_rd": [None, None], "code_rd": None,
                "acc_rd": [None, None], "wt_st": None}

        for ocn in range(NOC):
            ldq = p.dma("sync", lambda e, ocn=ocn: e.dma_start(
                out=q_all[:].rearrange("o (pl w) -> o pl w", w=128),
                in_=qw.ap()[:, ocn * 128:(ocn + 1) * 128, :].transpose([1, 0, 2])),
                deps=[prev["q_rd"]])
            ldl4 = p.dma("sync", lambda e, ocn=ocn: e.dma_start(
                out=lut4c[:], in_=lut4s.ap()[ocn * 128:(ocn + 1) * 128, :]),
                deps=[prev["lut_rd"][0]])
            ldl8 = p.dma("sync", lambda e, ocn=ocn: e.dma_start(
                out=lut8c[:], in_=lut8s.ap()[ocn * 128:(ocn + 1) * 128, :]),
                deps=[prev["lut_rd"][1]])
            cdep = []
            for s_ in range(32):
                sh = p.ins("vector", lambda e, s_=s_: e.tensor_single_scalar(
                    out=xsh[:], in_=q_all[:], scalar=s_, op=ALU.logical_shift_right),
                    deps=[ldq] + ([prev["code_rd"]] if s_ == 0 else []))
                an = p.ins("vector", lambda e: e.tensor_single_scalar(
                    out=xsh[:], in_=xsh[:], scalar=1, op=ALU.bitwise_and), deps=[sh])
                cv = p.ins("vector", lambda e: e.tensor_copy(out=bits[:], in_=xsh[:]),
                           deps=[an])
                mu = p.ins("vector", lambda e: e.tensor_tensor(
                    out=wgt[:], in0=bits[:], in1=p2t[:], op=ALU.mult),
                    deps=[cv, ld0[1]])
                rd = p.ins("vector", lambda e, s_=s_: e.reduce_sum(
                    out=bass.AP(code, s_, [[4096, 128], [32, 128]]),
                    in_=bass.AP(wgt, 0, [[1024, 128], [1, 128], [128, 8]]),
                    axis=AXX), deps=[mu])
                cdep.append(rd)
            cb = p.ins("vector", lambda e: e.tensor_copy(out=code_b[:], in_=code[:]),
                       deps=cdep)
            l1 = p.ins("vector", lambda e: e.tensor_copy(out=tmpi[:], in_=code[:]),
                       deps=cdep)
            l2 = p.ins("vector", lambda e: e.tensor_single_scalar(
                out=tmpi[:], in_=tmpi[:], scalar=15, op=ALU.bitwise_and), deps=[l1])
            l5 = p.ins("vector", lambda e: e.tensor_copy(out=lo_b[:], in_=tmpi[:]),
                       deps=[l2])
            prev["code_rd"] = l5
            for (nc_codes, lutc, lddep, src_b, accb, ai) in (
                    (16, lut4c, ldl4, lo_b, acc4, 0), (256, lut8c, ldl8, code_b, acc8, 1)):
                ms = p.ins("vector", lambda e, accb=accb: e.memset(accb[:], 0.0),
                           deps=[prev["acc_rd"][ai], l5 if ai == 0 else cb])
                dep = ms
                for cc in range(nc_codes):
                    e1 = p.ins("vector", lambda e, cc=cc, src_b=src_b:
                               e.tensor_single_scalar(
                        out=eq_b[:], in_=src_b[:], scalar=float(cc), op=ALU.is_equal),
                        deps=[dep] if cc == 0 else [])
                    e2 = p.ins("vector", lambda e, cc=cc, lutc=lutc, accb=accb:
                               e.scalar_tensor_tensor(
                        out=accb[:], in0=eq_b[:], scalar=lutc[:, cc:cc + 1],
                        in1=accb[:], op0=ALU.mult, op1=ALU.add),
                        deps=[lddep] if cc == 0 else [])
                    dep = e2
                std = []
                trcp = [None, None]
                pst = None
                for icg in range(8):
                    trs = []
                    for ii in range(4):
                        ic = icg * 4 + ii
                        ti_ = ii % 2
                        tr = p.ins("tensor", lambda e, ic=ic, ti_=ti_, accb=accb:
                                   e.transpose(
                            out=ps_tr[ti_][:, 0:128],
                            in_=accb[:, ic * 128:(ic + 1) * 128],
                            identity=ident_b[:]),
                            deps=([dep, ld0[0], prev["wt_st"]] if (icg == 0 and ii == 0)
                                  else [dep] if ii == 0 else []) + [trcp[ti_]])
                        cpi = p.ins("vector", lambda e, ii=ii, ti_=ti_: e.tensor_copy(
                            out=wtst[:, ii * 128:(ii + 1) * 128],
                            in_=ps_tr[ti_][:, 0:128]), deps=[tr, pst])
                        trcp[ti_] = cpi
                        trs.append(cpi)
                    cp = p.ins("vector", lambda e: e.engine_nop(), deps=trs)
                    wdst = WT4 if ai == 0 else WT8
                    st = p.dma("scalar", lambda e, icg=icg, ocn=ocn, wdst=wdst:
                               e.dma_start(
                        out=wdst.ap()[icg * 512:(icg + 1) * 512,
                                      ocn * 128:(ocn + 1) * 128]
                            .rearrange("(ic p) o -> p ic o", p=128),
                        in_=wtst[:].rearrange("p (ic o) -> p ic o", o=128)), deps=[cp])
                    std.append(st)
                    pst = st
                prev["acc_rd"][ai] = p.last("tensor")
                prev["wt_st"] = std
            prev["q_rd"] = cdep[-1]
            prev["lut_rd"] = [dep, dep]
        p.seal()
        p.build(block)
    return nc


def build_ffn_out(T_total=TPC):
    nc = bass.Bass("TRN2", target_bir_lowering=False, debug=False,
                   detect_race_conditions=RACE[0])
    x3T = nc.dram_tensor("x3T", [DM, T_total], F32, kind="ExternalInput")
    urow = nc.dram_tensor("urow", [1, T_total], F32, kind="ExternalInput")
    WT4 = nc.dram_tensor("WT4", [DM, 4096], BF16, kind="ExternalInput")
    WT8 = nc.dram_tensor("WT8", [DM, 4096], BF16, kind="ExternalInput")
    bias = nc.dram_tensor("bias", [128, 32], F32, kind="ExternalInput")
    onesf = nc.dram_tensor("onesf", [128, 128], F32, kind="ExternalInput")
    identf = nc.dram_tensor("identf", [128, 128], F32, kind="ExternalInput")
    outp = nc.dram_tensor("outp", [T_total, DM], F32, kind="ExternalOutput")

    TQ = 256
    NQ = T_total // TQ
    with ExitStack() as stack:
        E = stack.enter_context
        x3h = E(nc.sbuf_tensor("x3h", [128, NFC * TQ], F32))
        x3b = E(nc.sbuf_tensor("x3b", [128, NFC * TQ], BF16))
        w4b = [E(nc.sbuf_tensor("w4b%d" % i, [128, NFC * 128], BF16)) for i in range(2)]
        w8b = [E(nc.sbuf_tensor("w8b%d" % i, [128, NFC * 128], BF16)) for i in range(2)]
        b_s = E(nc.sbuf_tensor("b_s", [128, 32], F32))
        ones_f = E(nc.sbuf_tensor("ones_f", [128, 128], F32))
        ones_bf = E(nc.sbuf_tensor("ones_bf", [128, 1], BF16))
        ident_f = E(nc.sbuf_tensor("ident_f", [128, 128], F32))
        u_s = E(nc.sbuf_tensor("u_s", [1, 1024], F32))
        mrow = E(nc.sbuf_tensor("mrow", [1, 512], F32))
        mb_s = E(nc.sbuf_tensor("mb_s", [128, 256], F32))
        y4s = E(nc.sbuf_tensor("y4s", [128, 256], F32))
        dts = E(nc.sbuf_tensor("dts", [128, 256], F32))
        x4pre = E(nc.sbuf_tensor("x4pre", [128, NFC * TQ], F32))
        x4o = x3h
        xbf = E(nc.sbuf_tensor("xbf", [128, NFC * TQ], BF16))
        xsq = E(nc.sbuf_tensor("xsq", [128, NFC * TQ], BF16))
        rows = tuple(E(nc.sbuf_tensor("row%d" % i, [1, 512], F32)) for i in range(4))
        bc_sb = E(nc.sbuf_tensor("bc_sb", [128, 512], F32))
        tm_st = E(nc.sbuf_tensor("tm_st", [128, 4096], F32))
        ps_y4 = [E(nc.psum_tensor("ps_y4%d" % i, [128, 512], F32)) for i in range(2)]
        ps_y8 = [E(nc.psum_tensor("ps_y8%d" % i, [128, 512], F32)) for i in range(2)]
        ps_st = E(nc.psum_tensor("ps_st", [128, 512], F32))
        ps_bc = E(nc.psum_tensor("ps_bc", [128, 512], F32))
        ps_mb = ps_bc
        ps_tf = [E(nc.psum_tensor("ps_tfG%d" % i, [128, 512], F32)) for i in range(2)]
        p = Prog(nc, stack)
        block = E(nc.Block())

        ld0 = [p.dma("sync", lambda e: e.dma_start(out=b_s[:], in_=bias.ap())),
               p.dma("sync", lambda e: e.dma_start(out=ones_f[:], in_=onesf.ap())),
               p.dma("sync", lambda e: e.dma_start(out=ident_f[:], in_=identf.ap())),
               p.dma("sync", lambda e: e.dma_start(out=u_s[0:1, 0:T_total],
                                                   in_=urow.ap()))]
        onebf = p.ins("vector", lambda e: e.tensor_copy(out=ones_bf[:], in_=ones_f[:, 0:1]),
                      deps=[ld0[1]])

        prev = {"x3_rd": None, "w_rd": [None, None, None, None],
                "y4_rd": [None, None], "y8_rd": [None, None], "out_st": None,
                "ln_prev": None, "mb_rd": None, "tm_rd": None}

        for q in range(NQ):
            t0 = q * TQ
            ldx = p.dma("sync", lambda e, t0=t0: e.dma_start(
                out=x3h[:].rearrange("p (c t) -> p c t", t=TQ),
                in_=fm_slice_hbm(x3T, t0, TQ)), deps=[prev["x3_rd"]])
            cst = p.ins("vector", lambda e: e.tensor_copy(out=x3b[:], in_=x3h[:]),
                        deps=[ldx, p.last("tensor")])
            mr = p.ins("vector", lambda e, t0=t0: e.tensor_single_scalar(
                out=mrow[0:1, 0:TQ], in_=u_s[0:1, t0:t0 + TQ], scalar=1.0,
                op=ALU.is_gt), deps=[ld0[3], prev["mb_rd"]])
            mbm = p.ins("tensor", lambda e: e.matmul(
                out=ps_mb[:, 0:256], lhsT=ones_f[0:1, :], rhs=mrow[0:1, 0:TQ],
                start=True, stop=True), deps=[mr, ld0[1], prev["mb_rd"]])
            mbc = p.ins("vector", lambda e: e.tensor_copy(out=mb_s[:],
                                                          in_=ps_mb[:, 0:256]),
                        deps=[mbm])
            prev["mb_rd"] = mbc
            ev = None
            for oc in range(NFC):
                wi = oc % 2
                ldw4 = p.dma("sync", lambda e, wi=wi, oc=oc: e.dma_start(
                    out=w4b[wi][:].rearrange("p (c d) -> p c d", d=128),
                    in_=WT4.ap()[:, oc * 128:(oc + 1) * 128]
                        .rearrange("(c p) o -> p c o", p=128)),
                    deps=[prev["w_rd"][wi]])
                ldw8 = p.dma("sync", lambda e, wi=wi, oc=oc: e.dma_start(
                    out=w8b[wi][:].rearrange("p (c d) -> p c d", d=128),
                    in_=WT8.ap()[:, oc * 128:(oc + 1) * 128]
                        .rearrange("(c p) o -> p c o", p=128)),
                    deps=[prev["w_rd"][2 + wi]])
                mm4 = mm8 = None
                for c in range(NFC):
                    mm4 = p.ins("tensor", lambda e, wi=wi, c=c: e.matmul(
                        out=ps_y4[wi][:, 0:256],
                        lhsT=w4b[wi][:, c * 128:(c + 1) * 128],
                        rhs=x3b[:, c * TQ:(c + 1) * TQ],
                        start=(c == 0), stop=(c == NFC - 1)),
                        deps=[ldw4, cst, prev["y4_rd"][wi]] if c == 0 else [])
                for c in range(NFC):
                    mm8 = p.ins("tensor", lambda e, wi=wi, c=c: e.matmul(
                        out=ps_y8[wi][:, 0:256],
                        lhsT=w8b[wi][:, c * 128:(c + 1) * 128],
                        rhs=x3b[:, c * TQ:(c + 1) * TQ],
                        start=(c == 0), stop=(c == NFC - 1)),
                        deps=[ldw8, cst, prev["y8_rd"][wi]] if c == 0 else [])
                prev["w_rd"][wi] = mm4
                prev["w_rd"][2 + wi] = mm8
                ev0 = p.ins("scalar", lambda e, wi=wi: e.activation(
                    out=y4s[:], in_=ps_y4[wi][:, 0:256], func=AF.Copy),
                    deps=[mm4, prev["y4_rd"][wi]])
                prev["y4_rd"][wi] = ev0
                d1 = p.ins("vector", lambda e, wi=wi: e.tensor_tensor(
                    out=dts[:], in0=ps_y8[wi][:, 0:256], in1=y4s[:],
                    op=ALU.subtract), deps=[mm8, ev0])
                prev["y8_rd"][wi] = d1
                d2 = p.ins("vector", lambda e: e.tensor_tensor(
                    out=dts[:], in0=dts[:], in1=mb_s[:], op=ALU.mult),
                    deps=[d1, mbc])
                d3 = p.ins("vector", lambda e, oc=oc: e.scalar_tensor_tensor(
                    out=dts[:], in0=dts[:], scalar=b_s[:, oc:oc + 1], in1=y4s[:],
                    op0=ALU.add, op1=ALU.add), deps=[d2, ld0[0]])
                ev = p.ins("vector", lambda e, oc=oc: e.tensor_tensor(
                    out=x4pre[:, oc * TQ:(oc + 1) * TQ], in0=dts[:],
                    in1=x3h[:, oc * TQ:(oc + 1) * TQ], op=ALU.add),
                    deps=[d3, prev["ln_prev"]])
            apply_dep, _ = emit_ln(p, x4pre, x4o, xbf, xsq, ones_bf, ones_f, rows,
                                   ps_st, ps_bc, TQ,
                                   [ev, prev["out_st"], p.last("tensor")],
                                   bc_sb=bc_sb)
            sts = []
            trcp = [None, None]
            for tb in range(2):
                cps = []
                for c in range(NFC):
                    ti_ = c % 2
                    tr = p.ins("tensor", lambda e, c=c, ti_=ti_, tb=tb: e.transpose(
                        out=ps_tf[ti_][:, 0:128],
                        in_=x4o[:, c * TQ + tb * 128:c * TQ + (tb + 1) * 128],
                        identity=ident_f[:]),
                        deps=([apply_dep, ld0[2]]
                              if (tb == 0 and c == 0) else []) + [trcp[ti_]])
                    cp = p.ins("vector", lambda e, c=c, ti_=ti_: e.tensor_copy(
                        out=tm_st[:, c * 128:(c + 1) * 128], in_=ps_tf[ti_][:, 0:128]),
                        deps=[tr, prev["tm_rd"]] if c == 0 else [tr])
                    trcp[ti_] = cp
                    cps.append(cp)
                st = p.dma("scalar", lambda e, t0=t0, tb=tb: e.dma_start(
                    out=outp.ap()[t0 + tb * 128:t0 + (tb + 1) * 128, :],
                    in_=tm_st[:]), deps=cps)
                sts.append(st)
                prev["tm_rd"] = st
            prev["out_st"] = sts
            prev["ln_prev"] = [apply_dep, emit_ln.cast2]
            prev["x3_rd"] = p.last("tensor")
        p.seal()
        p.build(block)
    return nc


def _run(nc, in_maps):
    return run_bass_kernel_spmd(nc, in_maps, core_ids=list(range(NCORE)))


def _run_dev(nc, in_maps):
    return run_bass_kernel_spmd(nc, in_maps, core_ids=list(range(len(in_maps))))


def host_prep_consts():
    ones = np.ones((128, 128), np.float32)
    ident = _bf16(np.eye(128, dtype=np.float32))
    repl = np.zeros((128, NFC * 128), np.float32)
    for c in range(NFC):
        for fl in range(128):
            d = 4 * c + fl // 32
            repl[d, c * 128 + fl] = 1.0
    return ones, ident, _bf16(repl)


def kernel(**inp):
    x = np.asarray(inp["L0_attn_x"], np.float32)
    xT = np.ascontiguousarray(x.transpose(0, 2, 1))

    nc = build_prep()
    maps = []
    for c in range(NCORE):
        r0 = 512 * c
        maps.append({
            "wq0": np.ascontiguousarray(np.asarray(inp["L0_attn_Wq"], np.float32)
                                        .reshape(DM, 128 * 32)[r0:r0 + 512]),
            "wk0": np.ascontiguousarray(np.asarray(inp["L0_attn_Wk"], np.float32)
                                        .reshape(DM, 128)[r0:r0 + 512]),
            "wv0": np.ascontiguousarray(np.asarray(inp["L0_attn_Wv"], np.float32)
                                        .reshape(DM, 128)[r0:r0 + 512]),
            "wq1": np.ascontiguousarray(np.asarray(inp["L1_attn_Wq"], np.float32)[r0:r0 + 512]),
            "wk1": np.ascontiguousarray(np.asarray(inp["L1_attn_Wk"], np.float32)[r0:r0 + 512]),
            "wv1": np.ascontiguousarray(np.asarray(inp["L1_attn_Wv"], np.float32)[r0:r0 + 512]),
            "wo1": np.ascontiguousarray(np.asarray(inp["L1_attn_Wo"], np.float32)[r0:r0 + 512]),
        })
    res = _run(nc, maps)
    cat = lambda key: np.concatenate([r[key] for r in res.results], axis=0)
    wqs_bf, wk0_bf, wv0_bf = cat("wqs_bf"), cat("wk0_bf"), cat("wv0_bf")
    wq1_bf, wk1_bf, wv1_bf, wo1_bf = cat("wq1_bf"), cat("wk1_bf"), cat("wv1_bf"), cat("wo1_bf")

    ones, ident, repl = host_prep_consts()
    nc = build_l0attn()
    maps = []
    for c in range(NCORE):
        b, h = c // 2, c % 2
        maps.append({
            "xT": np.ascontiguousarray(xT[b][:, h * TPC:(h + 1) * TPC]),
            "wqs": wqs_bf, "wk0": wk0_bf, "wv0": wv0_bf,
            "bq0": np.ascontiguousarray(np.asarray(inp["L0_attn_bq"], np.float32)),
            "bk0": np.ascontiguousarray(np.asarray(inp["L0_attn_bk"], np.float32)),
            "bv0": np.ascontiguousarray(np.asarray(inp["L0_attn_bv"], np.float32)),
            "repl": repl, "onesf": ones, "identb": ident,
        })
    res = _run(nc, maps)
    x1T_c = [res.results[c]["x1T"] for c in range(NCORE)]

    identf = np.eye(128, dtype=np.float32)
    sexp = np.zeros((8, 128), np.float32)
    for e in range(8):
        sexp[e, e * 16:(e + 1) * 16] = 1.0
    nc = build_l0ffn()
    maps = []
    for c in range(NCORE):
        maps.append({
            "x1T": x1T_c[c],
            "routew": np.ascontiguousarray(np.asarray(inp["L0_ffn_route_w"], np.float32)),
            "aflat": np.ascontiguousarray(np.asarray(inp["L0_ffn_A"], np.float32)
                                          .reshape(128, DM)),
            "bw": np.ascontiguousarray(np.asarray(inp["L0_ffn_Bw"], np.float32)),
            "sexp": sexp, "onesf": ones, "identb": ident, "identf": identf,
        })
    res = _run(nc, maps)
    x2T_c = [res.results[c]["x2T"] for c in range(NCORE)]

    invf0 = (1.0 / 10000.0 ** (np.arange(0, 64, dtype=np.float32) / 64))
    invfr = np.concatenate([invf0, invf0]).reshape(1, 128).astype(np.float32)
    perm = np.zeros((128, 128), np.float32)
    for k2 in range(128):
        perm[k2, (k2 + 64) % 128] = 1.0
    permb = _bf16(perm)
    bq1 = np.ascontiguousarray(np.asarray(inp["L1_attn_bq"], np.float32).reshape(32, 128).T)
    bk1 = np.ascontiguousarray(np.asarray(inp["L1_attn_bk"], np.float32).reshape(8, 128).T)
    bv1 = np.ascontiguousarray(np.asarray(inp["L1_attn_bv"], np.float32).reshape(8, 128).T)
    nc = build_qkv()
    maps = []
    for c in range(NCORE):
        posr = np.arange((c % 2) * TPC, (c % 2 + 1) * TPC, dtype=np.float32).reshape(1, TPC)
        maps.append({"x2T": x2T_c[c], "wq1": wq1_bf, "wk1": wk1_bf, "wv1": wv1_bf,
                     "bq1": bq1, "bk1": bk1, "bv1": bv1, "invfr": invfr,
                     "posr": posr, "identb": ident, "permb": permb})
    res = _run(nc, maps)
    Qf_c = [res.results[c]["Qf"] for c in range(NCORE)]
    Kf_c = [res.results[c]["Kf"] for c in range(NCORE)]
    Vt_c = [res.results[c]["Vt"] for c in range(NCORE)]

    maskc = np.zeros((128, 128), np.float32)
    for qq in range(128):
        maskc[qq, qq + 1:] = -1e30
    maskb = _bf16(maskc)
    Qf_b = [np.concatenate([Qf_c[2 * b], Qf_c[2 * b + 1]], 1) for b in range(B)]
    Kf_b = [np.concatenate([Kf_c[2 * b], Kf_c[2 * b + 1]], 1) for b in range(B)]
    Vt_b = [np.concatenate([Vt_c[2 * b], Vt_c[2 * b + 1]], 1) for b in range(B)]
    nc = build_attn()
    maps = []
    for c in range(NCORE):
        b, j = c // 2, c % 2
        maps.append({"Qs": np.ascontiguousarray(Qf_b[b][2048 * j:2048 * (j + 1), :]),
                     "Ks": np.ascontiguousarray(Kf_b[b][512 * j:512 * (j + 1), :]),
                     "Vts": np.ascontiguousarray(Vt_b[b][4 * j:4 * (j + 1)]),
                     "identb": ident, "maskc": maskb, "onesf": ones})
    res = _run(nc, maps)
    ctxT_c = [res.results[c]["ctxT"] for c in range(NCORE)]
    ctxT_b = [np.concatenate([ctxT_c[2 * b], ctxT_c[2 * b + 1]], 0) for b in range(B)]

    bo1 = np.ascontiguousarray(np.asarray(inp["L1_attn_bo"], np.float32).reshape(32, 128).T)
    nc = build_wo_ln3()
    maps = []
    for c in range(NCORE):
        b, j = c // 2, c % 2
        maps.append({"ctxTs": np.ascontiguousarray(ctxT_b[b][:, j * TPC:(j + 1) * TPC]),
                     "wo1": wo1_bf, "bo1": bo1, "x2T": x2T_c[c], "onesf": ones})
    res = _run(nc, maps)
    x3T_c = [res.results[c]["x3T"] for c in range(NCORE)]
    urow_c = [res.results[c]["urow"] for c in range(NCORE)]

    p2 = np.zeros((128, 1024), np.float32)
    for pl in range(8):
        p2[:, pl * 128:(pl + 1) * 128] = float(2 ** pl)
    qwf = np.asarray(inp["L1_ffn_qweight"])
    lut4f = np.asarray(inp["L1_ffn_lut4"], np.float32)
    lut8f = np.asarray(inp["L1_ffn_lut8"], np.float32)
    nc = build_dequant()
    maps = []
    for c in range(NCORE):
        r0 = 512 * c
        maps.append({"qw": np.ascontiguousarray(qwf[:, r0:r0 + 512]),
                     "lut4s": np.ascontiguousarray(lut4f[r0:r0 + 512]),
                     "lut8s": np.ascontiguousarray(lut8f[r0:r0 + 512]),
                     "identb": ident, "p2t": p2})
    res = _run(nc, maps)
    WT4 = np.concatenate([res.results[c]["WT4"] for c in range(NCORE)], 1)
    WT8 = np.concatenate([res.results[c]["WT8"] for c in range(NCORE)], 1)

    biasg = np.ascontiguousarray(np.asarray(inp["L1_ffn_bias"], np.float32)
                                 .reshape(32, 128).T)
    nc = build_ffn_out()
    maps = []
    for c in range(NCORE):
        maps.append({"x3T": x3T_c[c], "urow": urow_c[c], "WT4": WT4, "WT8": WT8,
                     "bias": biasg, "onesf": ones, "identf": identf})
    res = _run(nc, maps)
    outs = [np.asarray(res.results[c]["outp"], np.float32) for c in range(NCORE)]
    final = np.stack([np.concatenate([outs[2 * b], outs[2 * b + 1]], 0)
                      for b in range(B)])
    return np.ascontiguousarray(final)
